# version 25
# baseline (speedup 1.0000x reference)
import numpy as np
import ml_dtypes

import concourse.bass as bass
import concourse.mybir as mybir
from concourse.bass_utils import run_bass_kernel_spmd

F32 = mybir.dt.float32
BF16 = mybir.dt.bfloat16
I32 = mybir.dt.int32
AF = mybir.ActivationFunctionType
ALU = mybir.AluOpType
AX = mybir.AxisListType

E = 4096
NE = 32
KV = 8
OC = 32
RSCALE = float(1.0 / np.float32(np.sqrt(np.float32(128.0))))
P2N = 6

ENGS = ("sp", "act", "pe", "dve", "pool")


def A(t, offset, *dims):
    return bass.AP(t, offset, [list(d) for d in dims])


class Buf:
    __slots__ = ("name", "arena", "lo", "hi", "writer", "readers")

    def __init__(self, name, arena, lo, hi):
        self.name, self.arena, self.lo, self.hi = name, arena, lo, hi
        self.writer = None
        self.readers = []


class KB:

    def __init__(self):
        self.recs = []
        self.ccount = {e: 0 for e in ENGS}
        self.dcount = {}
        self.floors = {e: {} for e in ENGS}
        self.arenas = {}

    def buf(self, name, arena=None, lo=0, hi=1 << 30):
        b = Buf(name, arena if arena is not None else name, lo, hi)
        self.arenas.setdefault(b.arena, []).append(b)
        return b

    def _conf(self, b):
        for o in self.arenas[b.arena]:
            if o is not b and o.lo < b.hi and o.hi > b.lo:
                yield o

    def op(self, eng, fn, reads=(), writes=(), dma=False, cc=False):
        need = {}

        def req(dep):
            if dep and need.get(dep[0], 0) < dep[1]:
                need[dep[0]] = dep[1]

        for rb in reads:
            req(rb.writer)
            for o in self._conf(rb):
                req(o.writer)
        for wb in writes:
            req(wb.writer)
            for rd in wb.readers:
                req(rd)
            for o in self._conf(wb):
                req(o.writer)
                for rd in o.readers:
                    req(rd)
        waits = []
        fl = self.floors[eng]
        for sem, val in sorted(need.items(), key=lambda kv: str(kv[0])):
            if fl.get(sem, 0) >= val:
                continue
            if sem == ("c", eng) and val > self.ccount[eng]:
                raise RuntimeError(f"future dep on own engine {eng}")
            waits.append((sem, val))
            fl[sem] = val
        if cc:
            key = ("cc", eng)
            self.dcount[key] = self.dcount.get(key, 0) + 1
            mykey = (key, self.dcount[key])
            incs = [(key, 1)]
        elif dma:
            key = ("d", dma)
            self.dcount[key] = self.dcount.get(key, 0) + 16
            mykey = (key, self.dcount[key])
            incs = [(key, 16)]
        else:
            self.ccount[eng] += 1
            mykey = (("c", eng), self.ccount[eng])
            incs = [(("c", eng), 1)]
        self.recs.append((eng, fn, waits, incs))
        for wb in writes:
            wb.writer = mykey
            wb.readers = []
        for rb in reads:
            rb.readers.append(mykey)
        return mykey

    def emit(self, block, sems):
        def section(e, name):
            for eng, fn, waits, incs in self.recs:
                if eng != name:
                    continue
                for semkey, val in waits:
                    e.wait_ge(sems[semkey], val)
                inst = fn(e)
                for semkey, amt in incs:
                    inst.then_inc(sems[semkey], amt)

        @block.sync
        def _(e):
            section(e, "sp")

        @block.vector
        def _(e):
            section(e, "dve")

        @block.scalar
        def _(e):
            section(e, "act")

        @block.tensor
        def _(e):
            section(e, "pe")

        @block.gpsimd
        def _(e):
            section(e, "pool")


def build_program(T=1024, ncores=8, phases="abcd12345"):
    NT = T // 128
    HW = min(512, T)
    NH = T // HW
    NJT = OC // ncores
    NXH = 2 if T >= 1024 else 1
    TH = T // NXH
    nc = bass.Bass("TRN2", target_bir_lowering=False)

    d_x = nc.dram_tensor("x_in", [T, E], F32, kind="ExternalInput").ap()
    d_wq = nc.dram_tensor("wq_p", [KV, 8, 128, 2048], F32, kind="ExternalInput").ap()
    d_bq = nc.dram_tensor("bq_in", [128, 32], F32, kind="ExternalInput").ap()
    d_wk = nc.dram_tensor("wk_p", [KV, 128, 4096], F32, kind="ExternalInput").ap()
    d_bk = nc.dram_tensor("bk_in", [128, 8], F32, kind="ExternalInput").ap()
    d_wv = nc.dram_tensor("wv_p", [KV, 128, 4096], F32, kind="ExternalInput").ap()
    d_bv = nc.dram_tensor("bv_in", [128, 8], F32, kind="ExternalInput").ap()
    d_pk = nc.dram_tensor("pk_in", [128 * NJT, 2048], I32, kind="ExternalInput").ap()
    d_sc = nc.dram_tensor("sc_in", [128 * NJT, 256], F32, kind="ExternalInput").ap()
    d_gs = nc.dram_tensor("gs_in", [1, 1], F32, kind="ExternalInput").ap()
    d_a = nc.dram_tensor("a_in", [16, E], F32, kind="ExternalInput").ap()
    d_b = nc.dram_tensor("b_in", [128, 512], F32, kind="ExternalInput").ap()
    d_y = nc.dram_tensor("y_out", [T, E], F32, kind="ExternalOutput").ap()

    d_ccin = nc.dram_tensor("cc_in", [NJT * 128, 4096], BF16).ap()
    d_ccout = nc.dram_tensor("cc_out", [OC * 128, 4096], BF16).ap()
    d_sg = nc.dram_tensor("sg_scratch", [T, OC * 128], F32).ap()

    kb = KB()
    guards = []

    def sb(name, shape, dt):
        g = nc.sbuf_tensor(name, shape, dt)
        guards.append(g)
        return g.__enter__()

    def at(base_h, name, shape, dt, off):
        addr = nc.lookup_mloc(base_h).addr + off
        return nc.alloc_sbuf_tensor_at(name, shape, dt, offset=addr)

    def psum(name, shape, dt):
        g = nc.psum_tensor(name, shape, dt)
        guards.append(g)
        return g.__enter__()

    XB = max(NE * T * 2, 49152)
    xar = sb("xar", [128, XB], mybir.dt.uint8)
    xnTh = [at(xar, f"xnTh{i}", [128, NE * TH], BF16, i * NE * TH * 2) for i in range(NXH)]
    sgT = [at(xar, f"sgTt{i}", [128, E], F32, i * 16384) for i in range(2)]
    outh = [at(xar, f"outh{i}", [128, 2048], F32, 32768 + i * 8192) for i in range(2)]
    xt = sb("xt", [128, E], F32)
    xnb = sb("xnb", [128, E], BF16)
    qI = at(xt, "qI", [128, 4096], BF16, 0)
    kI = at(xt, "kI", [128, 4096], BF16, 8192)
    qsl = sb("qsl", [128, KV * T], BF16)
    if KV * T * 2 < 16384:
        sb("qslpad", [128, (16384 - KV * T * 2) // 2], BF16)
    xt2 = at(qsl, "xt2", [128, E], F32, 0)
    ksl = sb("ksl", [128, KV * T], BF16)
    if KV * T * 2 < 16384:
        sb("kslpad", [128, (16384 - KV * T * 2) // 2], BF16)
    vsl = sb("vsl", [128, 9 * T], BF16)
    if 9 * T * 2 < 8192:
        sb("vslpad", [128, (8192 - 9 * T * 2) // 2], BF16)
    junkb = at(vsl, "junkb", [128, E], BF16, 0)
    zar = sb("zar", [128, 9216], F32)
    ssl = at(zar, "ssl", [128, KV * T], F32, 0)
    pkt = at(zar, "pkt", [128, 2048], I32, 0)
    iscr = [at(zar, f"iscr{i}", [128, 512], I32, 8192 + i * 2048) for i in range(5)]
    wdec = at(zar, "wdec", [128, 4096], BF16, 18432)
    wtb = at(zar, "wtb", [128, 4096], BF16, 26624)
    lhsT = [sb("lhsA", [128, E], BF16), sb("lhsB", [128, E], BF16)]
    xtp = at(ksl, "xtp", [128, E], F32, 0)
    xtv = at(vsl, "xtv", [128, E], F32, 0) if 9 * T * 2 >= 16384 else None
    fold3 = [at(ksl, "fold3_0", [128, 2048], F32, 0),
             at(ksl, "fold3_1", [128, 2048], F32, 8192),
             at(vsl, "fold3_2", [128, 2048], F32, 0)]
    qksS = at(zar, "qksS", [64, 2048], BF16, 32768)
    expTS = sb("expTS", [128, 2048], BF16)
    rc = sb("rc", [128, 16], F32)
    lm = sb("lm", [16, T], BF16)
    At = sb("At", [128, 512], BF16)
    Bt = sb("Bt", [16, E], BF16)
    bqsb = sb("bqsb", [128, 32], F32)
    bqt = sb("bqt", [128, 8], F32)
    bkv = sb("bkv", [128, 16], F32)
    gbc = sb("gbc", [128, 1], F32)
    ones1 = sb("ones1", [1, 128], F32)
    scs = at(zar, "scs", [128, 256], F32, 34816)
    scs2 = at(zar, "scs2", [128, 256], F32, 35840)
    asb = at(xar, "asb", [16, E], F32, 0)
    bsb = at(xar, "bsb", [128, 512], F32, 16384)
    idb = sb("idb", [128, 128], BF16)
    idf = sb("idf", [128, 128], F32)
    gsb = at(qsl, "gsb", [128, 1024], F32, 0)
    sgs = at(qsl, "sgs", [128, 1024], F32, 4096)
    st = sb("st", [128, 16], F32)

    pb = [psum("pb0", [128, 512], F32), psum("pb1", [128, 512], F32),
          psum("pb2", [128, 1024], BF16), psum("pb3", [128, 1024], BF16)]
    scd = [psum("scd0", [128, 1024], F32), psum("scd1", [128, 1024], F32)]

    b_xnTh = [kb.buf(f"xnTh{i}", "X", i * NE * TH * 2, (i + 1) * NE * TH * 2)
              for i in range(NXH)]
    b_sgT = [kb.buf(f"sgT{i}", "X", i * 16384, (i + 1) * 16384) for i in range(2)]
    b_outh = [kb.buf(f"outh{i}", "X", 32768 + i * 8192, 32768 + (i + 1) * 8192)
              for i in range(2)]
    b_xt = kb.buf("xt", "XT", 0, 16384)
    b_xt2 = kb.buf("xt2", "QSL", 0, 16384)
    b_junk = kb.buf("junkb", "VSL", 0, 8192)
    b_xnb = kb.buf("xnb")
    b_qI = kb.buf("qI", "XT", 0, 8192)
    b_kI = kb.buf("kI", "XT", 8192, 16384)
    b_qsl = kb.buf("qsl", "QSL", 0, 16384)
    b_ksl = kb.buf("ksl", "KSL", 0, 16384)
    b_vsl = kb.buf("vsl", "VSL", 0, 18432)
    b_ssl = kb.buf("ssl", "Z", 0, 4 * KV * T)
    b_pk = kb.buf("pk", "Z", 0, 8192)
    b_int = [kb.buf(f"i{i}", "Z", 8192 + i * 2048, 8192 + (i + 1) * 2048) for i in range(5)]
    b_wdec = kb.buf("wdec", "Z", 18432, 26624)
    b_wtb = kb.buf("wtb", "Z", 26624, 34816)
    b_lhs = [kb.buf("lhsA"), kb.buf("lhsB")]
    b_xtp = kb.buf("xtp", "KSL", 0, 16384)
    b_xtv = kb.buf("xtv", "VSL", 0, 16384)
    b_fold = [kb.buf("fold0", "KSL", 0, 8192), kb.buf("fold1", "KSL", 8192, 16384),
              kb.buf("fold2", "VSL", 0, 8192)]
    b_qksS = [kb.buf(f"qksS{i}", "Z", 32768 + i * 2048, 32768 + (i + 1) * 2048)
              for i in range(2)]
    b_expTS = [kb.buf(f"expTS{i}", "expTS", i * 2048, (i + 1) * 2048) for i in range(2)]
    b_rc = [kb.buf(f"rc{i}", "rc", i * 32, (i + 1) * 32) for i in range(2)]
    b_lm = kb.buf("lm")
    b_At, b_Bt = kb.buf("At"), kb.buf("Bt")
    b_bqsb, b_bqt, b_bkv = kb.buf("bqsb"), kb.buf("bqt"), kb.buf("bkv")
    b_gbc = kb.buf("gbc")
    b_scs = kb.buf("scs", "Z", 34816, 35840)
    b_scs2 = kb.buf("scs2", "Z", 35840, 36864)
    b_ones1 = kb.buf("ones1")
    b_asb = kb.buf("asb", "X", 0, 16384)
    b_bsb = kb.buf("bsb", "X", 16384, 18432)
    b_idb, b_idf = kb.buf("idb"), kb.buf("idf")
    b_g = [kb.buf(f"g{i}", "QSL", i * 2048, (i + 1) * 2048) for i in range(2)]
    b_sge = [kb.buf(f"sge{i}", "QSL", i * 2048, (i + 1) * 2048) for i in range(2)]
    b_sg = [kb.buf(f"sg{i}", "QSL", 4096 + i * 2048, 4096 + (i + 1) * 2048)
            for i in range(2)]
    b_st2 = [kb.buf(f"st{i}", "ST", i * 32, (i + 1) * 32) for i in range(2)]
    b_pb = [kb.buf(f"pb{i}") for i in range(4)]
    b_scdA = [kb.buf(f"scd{i}A", f"SCD{i}", 0, 2048) for i in range(2)]
    b_scdB = [kb.buf(f"scd{i}B", f"SCD{i}", 2048, 4096) for i in range(2)]
    b_ccin = [kb.buf(f"ccin{j}", "CCIN", j, j + 1) for j in range(NJT)]
    b_ccout = kb.buf("ccout")
    b_sgscr = [kb.buf(f"sgscr{c}", "SGSCR", c, c + 1) for c in range(OC)]
    b_ydma = [kb.buf(f"y{i}", "YOUT", i, i + 1) for i in range(2 * NT)]

    XP = NE * T

    kb.op("pool", lambda e: e.memset(idb[:, :], 0.0), writes=[b_idb])
    kb.op("pool", lambda e: e.affine_select(
        out=idb[:, :], in_=idb[:, :], compare_op=ALU.not_equal, fill=1.0,
        base=0, pattern=[[-1, 128]], channel_multiplier=1),
        reads=[b_idb], writes=[b_idb])
    kb.op("pool", lambda e: e.memset(idf[:, :], 0.0), writes=[b_idf])
    kb.op("pool", lambda e: e.affine_select(
        out=idf[:, :], in_=idf[:, :], compare_op=ALU.not_equal, fill=1.0,
        base=0, pattern=[[-1, 128]], channel_multiplier=1),
        reads=[b_idf], writes=[b_idf])

    kb.op("sp", lambda e: e.dma_start(out=bqsb[:, :], in_=d_bq), writes=[b_bqsb], dma="bqsb")
    kb.op("dve", lambda e: e.tensor_reduce(
        bqt[:, :].rearrange("p (a b) -> p a b", b=1),
        bqsb[:, :].rearrange("p (a b) -> p a b", b=4), AX.X, ALU.add),
        reads=[b_bqsb], writes=[b_bqt])
    kb.op("sp", lambda e: e.dma_start(out=bkv[:, 0:8], in_=d_bk), writes=[b_bkv], dma="bkv")
    kb.op("sp", lambda e: e.dma_start(out=bkv[:, 8:16], in_=d_bv), writes=[b_bkv], dma="bkv")
    kb.op("sp", lambda e: e.dma_start(out=gbc[0:1, :], in_=d_gs), writes=[b_gbc], dma="gs")
    kb.op("pool", lambda e: e.memset(ones1[0:1, :], 1.0), writes=[b_ones1])
    kb.op("pe", lambda e: e.matmul(A(scd[1], 0, (1024, 128), (1, 1)), ones1[0:1, :],
                                   gbc[0:1, 0:1], start=True, stop=True),
          reads=[b_ones1, b_gbc], writes=[b_scdA[1]])
    kb.op("dve", lambda e: e.tensor_copy(gbc[:, :], A(scd[1], 0, (1024, 128), (1, 1))),
          reads=[b_scdA[1]], writes=[b_gbc])

    kb.op("sp", lambda e: e.dma_start(out=asb[:, :], in_=d_a), writes=[b_asb], dma="asb")
    for k in range(NE):
        kb.op("pe", lambda e, k=k: e.transpose(
            A(scd[1], 16 * k, (1024, 128), (1, 16)),
            asb[:, 128 * k:128 * (k + 1)], idf[0:16, 0:16]),
            reads=[b_asb, b_idf], writes=[b_scdA[1]])
    kb.op("dve", lambda e: e.tensor_copy(At[:, :], A(scd[1], 0, (1024, 128), (1, 512))),
          reads=[b_scdA[1]], writes=[b_At])

    kb.op("sp", lambda e: e.dma_start(out=bsb[:, :], in_=d_b),
          writes=[b_bsb], dma="bsb")
    for t in range(NE):
        kb.op("pe", lambda e, t=t: e.transpose(
            A(scd[1], 512 + 128 * (t % 4), (1024, 16), (1, 128)),
            bsb[:, 16 * t:16 * (t + 1)], idf[:, :]),
            reads=[b_bsb, b_idf], writes=[b_scdB[1]])
        if t % 4 == 3:
            def ev_b(e, t=t):
                last = None
                for u in range(t - 3, t + 1):
                    last = e.tensor_copy(
                        A(Bt, 4 * u, (E, 16), (1, 4), (128, 32)),
                        A(scd[1], 512 + 128 * (u % 4), (1024, 16), (32, 4), (1, 32)))
                return last
            kb.op("dve", ev_b, reads=[b_scdB[1]], writes=[b_Bt])

    def p0c_load(tt):
        xp = tt % 2
        xb, bxb = (xt, b_xt) if xp == 0 else (xt2, b_xt2)
        so = 8 * xp
        bst = b_st2[xp]
        kb.op("sp", lambda e, tt=tt, xb=xb: e.dma_start(
            out=xb[:, :], in_=d_x[128 * tt:128 * (tt + 1), :]),
            writes=[bxb], dma=f"xt{xp}")
        kb.op("act", lambda e, xb=xb, so=so: e.activation(
            junkb[:, :], xb[:, :], AF.Identity, accum_out=st[:, so:so + 1]),
            reads=[bxb], writes=[b_junk, bst])
        kb.op("act", lambda e, xb=xb, so=so: e.activation(
            junkb[:, :], xb[:, :], AF.Square, accum_out=st[:, so + 1:so + 2]),
            reads=[bxb], writes=[b_junk, bst])

    def p0c_finish(tt):
        xp = tt % 2
        xb, bxb = (xt, b_xt) if xp == 0 else (xt2, b_xt2)
        so = 8 * xp
        b_st = b_st2[xp]
        kb.op("dve", lambda e, so=so: e.tensor_scalar(
            st[:, so + 2:so + 3], st[:, so:so + 1], 1.0 / E, None, ALU.mult),
            reads=[b_st], writes=[b_st])
        kb.op("dve", lambda e, so=so: e.tensor_scalar(
            st[:, so + 3:so + 4], st[:, so + 2:so + 3], st[:, so + 2:so + 3],
            -1e-5, ALU.mult, ALU.add),
            reads=[b_st], writes=[b_st])
        kb.op("dve", lambda e, so=so: e.scalar_tensor_tensor(
            st[:, so + 4:so + 5], st[:, so + 1:so + 2], 1.0 / E,
            st[:, so + 3:so + 4], ALU.mult, ALU.subtract),
            reads=[b_st], writes=[b_st])
        kb.op("act", lambda e, so=so: e.activation(
            st[:, so + 5:so + 6], st[:, so + 4:so + 5], AF.Ln),
            reads=[b_st], writes=[b_st])
        kb.op("act", lambda e, so=so: e.activation(
            st[:, so + 6:so + 7], st[:, so + 5:so + 6], AF.Exp, scale=-0.5),
            reads=[b_st], writes=[b_st])
        kb.op("dve", lambda e, xb=xb, so=so: e.tensor_scalar(
            xnb[:, :], xb[:, :], st[:, so + 2:so + 3], st[:, so + 6:so + 7],
            ALU.subtract, ALU.mult),
            reads=[bxb, b_st], writes=[b_xnb])
        hh, tloc = (tt * NXH) // NT, tt % (NT // NXH)
        for k in range(NE):
            bi = 2 + (k // 8) % 2
            kb.op("pe", lambda e, k=k, bi=bi: e.transpose(
                A(pb[bi], 128 * (k % 8), (1024, 128), (1, 128)),
                xnb[:, 128 * k:128 * (k + 1)], idb[:, :]),
                reads=[b_xnb, b_idb], writes=[b_pb[bi]])
            if k % 8 == 7:
                kb.op("dve", lambda e, k=k, bi=bi, hh=hh, tloc=tloc: e.tensor_copy(
                    A(xnTh[hh], (k - 7) * TH + tloc * 128, (NE * TH, 128), (TH, 8), (1, 128)),
                    pb[bi][:, :]),
                    reads=[b_pb[bi]], writes=[b_xnTh[hh]])

    if "c" in phases:
        prev = None
        for tt in range(NT):
            p0c_load(tt)
            if prev is not None:
                p0c_finish(prev)
            prev = tt
        p0c_finish(prev)

    dec_pieces = []

    def _dec_load(jt):
        kb.op("pool", lambda e, jt=jt: e.dma_start(
            out=pkt[:, :], in_=d_pk[128 * jt:128 * (jt + 1), :]),
            reads=[b_ssl], writes=[b_pk], dma="pk")
        kb.op("pool", lambda e, jt=jt: e.dma_start(
            out=scs[:, :], in_=d_sc[128 * jt:128 * (jt + 1), :]),
            writes=[b_scs], dma="scs")
        kb.op("dve", lambda e: e.tensor_scalar(
            scs2[:, :], scs[:, :], gbc[:, 0:1], None, ALU.mult),
            reads=[b_scs, b_gbc], writes=[b_scs2])

    def _dec_sub(jt, s4):
        for ai, off in ((0, 0), (1, 1)):
            ic, isb = (0, 1) if ai == 0 else (3, 4)
            pks = pkt[:, 512 * s4:512 * (s4 + 1)]
            if ai == 0:
                kb.op("dve", lambda e, pks=pks, ic=ic: e.tensor_scalar(
                    iscr[ic][:, :], pks, 7, None, ALU.bitwise_and),
                    reads=[b_pk], writes=[b_int[ic]])
                kb.op("dve", lambda e, pks=pks, isb=isb: e.tensor_scalar(
                    iscr[isb][:, :], pks, 8, 28,
                    ALU.bitwise_and, ALU.logical_shift_left),
                    reads=[b_pk], writes=[b_int[isb]])
            else:
                kb.op("dve", lambda e, pks=pks, ic=ic: e.tensor_scalar(
                    iscr[ic][:, :], pks, 4, 7,
                    ALU.logical_shift_right, ALU.bitwise_and),
                    reads=[b_pk], writes=[b_int[ic]])
                kb.op("dve", lambda e, pks=pks, isb=isb: e.tensor_scalar(
                    iscr[isb][:, :], pks, 128, 24,
                    ALU.bitwise_and, ALU.logical_shift_left),
                    reads=[b_pk], writes=[b_int[isb]])
            kb.op("pool", lambda e, ic=ic: e.tensor_scalar(
                iscr[2][:, :], iscr[ic][:, :], 4194304, 1056964608,
                ALU.mult, ALU.add),
                reads=[b_int[ic]], writes=[b_int[2]])
            kb.op("dve", lambda e, ic=ic: e.scalar_tensor_tensor(
                iscr[2][:, :], iscr[ic][:, :], 1056964608, iscr[2][:, :],
                ALU.mult, ALU.min),
                reads=[b_int[ic], b_int[2]], writes=[b_int[2]])
            kb.op("dve", lambda e, ic=ic, isb=isb: e.tensor_tensor(
                iscr[ic][:, :], iscr[2][:, :], iscr[isb][:, :], ALU.bitwise_or),
                reads=[b_int[2], b_int[isb]], writes=[b_int[ic]])
            kb.op("pool", lambda e, s4=s4, off=off, ic=ic: e.tensor_tensor(
                A(wdec, 1024 * s4 + off, (4096, 128), (16, 64), (2, 8)),
                A(iscr[ic], 0, (512, 128), (8, 64), (1, 8)).bitcast(F32),
                A(scs2, 64 * s4, (256, 128), (1, 64), (0, 8)),
                ALU.mult),
                reads=[b_int[ic], b_scs2], writes=[b_wdec])

    def _dec_fin(jt):
        for k in range(NE):
            bi = 2 + (k // 8) % 2
            kb.op("pe", lambda e, k=k, bi=bi: e.transpose(
                A(pb[bi], 128 * (k % 8), (1024, 128), (1, 128)),
                wdec[:, 128 * k:128 * (k + 1)], idb[:, :]),
                reads=[b_wdec, b_idb], writes=[b_pb[bi]])
            if k % 8 == 7:
                kb.op("dve", lambda e, k=k, bi=bi: e.tensor_copy(
                    wtb[:, 128 * (k - 7):128 * (k + 1)], pb[bi][:, :]),
                    reads=[b_pb[bi]], writes=[b_wtb])
        kb.op("pool", lambda e, jt=jt: e.dma_start(
            out=d_ccin[128 * jt:128 * (jt + 1), :], in_=wtb[:, :]),
            reads=[b_wtb], writes=[b_ccin[jt]], dma="ccin")

    if "b" in phases:
        for jt in range(NJT):
            dec_pieces.append(lambda jt=jt: _dec_load(jt))
            for s4 in range(4):
                dec_pieces.append(lambda jt=jt, s4=s4: _dec_sub(jt, s4))
            dec_pieces.append(lambda jt=jt: _dec_fin(jt))

    pari = 0
    for kind in (("q", "k", "v") if "1" in phases else ()):
        for kv in range(KV):
            par = pari % 2
            pari += 1
            if kind == "q":
                for kq in range(8):
                    fp = kq % 3
                    kb.op("sp", lambda e, kv=kv, kq=kq, fp=fp: e.dma_start(
                        out=A(fold3[fp], 0, (2048, 128), (1, 2048)),
                        in_=d_wq[kv, kq]),
                        writes=[b_fold[fp]], dma=f"fold{fp}")
                    kb.op("dve", lambda e, kq=kq, par=par, fp=fp: e.tensor_reduce(
                        A(lhsT[par], 512 * kq, (E, 128), (128, 4), (1, 128), (1, 1)),
                        A(fold3[fp], 0, (2048, 128), (512, 4), (4, 128), (1, 4)),
                        AX.X, ALU.add),
                        reads=[b_fold[fp]], writes=[b_lhs[par]])
            else:
                if kind == "v" and kv == 0:
                    kb.op("pool", lambda e: e.memset(vsl[:, 8 * T:9 * T], 1.0),
                          writes=[b_vsl])
                src_ = d_wk if kind == "k" else d_wv
                kb.op("pool", lambda e, kv=kv, par=par, src_=src_: e.dma_start(
                    out=lhsT[par][:, :], in_=src_[kv]),
                    writes=[b_lhs[par]], dma=f"lhsp{par}")
            if dec_pieces:
                dec_pieces.pop(0)()
            for half in range(NH):
                def pj_fn(e, par=par, half=half):
                    last = None
                    for k in range(NE):
                        last = e.matmul(
                            A(pb[half], 0, (512, 128), (1, HW)),
                            lhsT[par][:, 128 * k:128 * (k + 1)],
                            A(xnTh[half % NXH], k * TH, (NE * TH, 128), (1, HW)),
                            start=(k == 0), stop=(k == NE - 1))
                    return last
                kb.op("pe", pj_fn, reads=[b_lhs[par], b_xnTh[half % NXH]],
                      writes=[b_pb[half]])
                if kind == "q":
                    dst, bias, bb, brd = qsl, bqt[:, kv:kv + 1], b_qsl, b_bqt
                elif kind == "k":
                    dst, bias, bb, brd = ksl, bkv[:, kv:kv + 1], b_ksl, b_bkv
                else:
                    dst, bias, bb, brd = vsl, bkv[:, 8 + kv:9 + kv], b_vsl, b_bkv
                kb.op("act", lambda e, kv=kv, half=half, dst=dst, bias=bias: e.activation(
                    dst[:, kv * T + half * HW: kv * T + (half + 1) * HW],
                    A(pb[half], 0, (512, 128), (1, HW)), AF.Identity, bias=bias),
                    reads=[b_pb[half], brd], writes=[bb])

    for half in (range(NH) if "1" in phases else ()):
        def lm_fn(e, half=half):
            last = None
            for k in range(NE):
                last = e.matmul(
                    A(scd[half], 0, (1024, 16), (1, HW)),
                    At[:, 16 * k:16 * (k + 1)],
                    A(xnTh[half % NXH], k * TH, (NE * TH, 128), (1, HW)),
                    start=(k == 0), stop=(k == NE - 1))
            return last
        kb.op("pe", lm_fn, reads=[b_At, b_xnTh[half % NXH]],
              writes=[b_scdA[half]])
        kb.op("act", lambda e, half=half: e.copy(
            lm[:, half * HW:(half + 1) * HW], A(scd[half], 0, (1024, 16), (1, HW))),
            reads=[b_scdA[half]], writes=[b_lm])

    while dec_pieces:
        dec_pieces.pop(0)()
    if "b" in phases:
        kb.op("pool", lambda e: e.collective_compute(
            "AllGather", ALU.bypass, replica_groups=[list(range(ncores))],
            ins=[d_ccin.opt()], outs=[d_ccout.opt()]),
            reads=b_ccin, writes=[b_ccout], cc=True)

    if "2" in phases:
        kb.op("pool", lambda e: e.memset(qI[:, :], 0.0), writes=[b_qI])
        kb.op("pool", lambda e: e.memset(kI[:, :], 0.0), writes=[b_kI])
    for ccu in (range(NT) if "2" in phases else ()):
        t0 = ccu * 128
        kb.op("dve", lambda e, t0=t0: e.tensor_copy(
            A(qI, 0, (4096, 128), (32, 128), (1, 8)),
            A(qsl, t0, (KV * T, 128), (1, 128), (T, 8))),
            reads=[b_qsl], writes=[b_qI])
        kb.op("dve", lambda e, t0=t0: e.tensor_copy(
            A(kI, 0, (4096, 128), (32, 128), (1, 8)),
            A(ksl, t0, (KV * T, 128), (1, 128), (T, 8))),
            reads=[b_ksl], writes=[b_kI])
        def p2_tp(sg):
            p2 = sg % 2
            bi = 2 + p2
            def tp8_fn(e, sg=sg, bi=bi):
                last = None
                for j in range(4):
                    g = 4 * sg + j
                    e.transpose(A(pb[bi], 256 * j, (1024, 64), (1, 128)),
                                kI[:, 64 * g:64 * g + 64], idb[:, :])
                    last = e.transpose(A(pb[bi], 256 * j + 128, (1024, 64), (1, 128)),
                                       qI[:, 64 * g:64 * g + 64], idb[:, :])
                return last
            kb.op("pe", tp8_fn, reads=[b_kI, b_qI, b_idb], writes=[b_pb[bi]])
            if P2N < 2:
                return
            kb.op("dve", lambda e, p2=p2, bi=bi: e.tensor_copy(
                qksS[0:64, 1024 * p2:1024 * (p2 + 1)],
                A(pb[bi], 0, (1024, 64), (1, 1024))),
                reads=[b_pb[bi]], writes=[b_qksS[p2]])

        def p2_sc(sg):
            p2 = sg % 2
            if P2N < 3:
                return
            def sc8_fn(e, p2=p2):
                last = None
                for j in range(4):
                    for t2 in range(2):
                        last = e.matmul(
                            A(scd[p2], 512 * t2 + 128 * j, (1024, 128), (1, 128)),
                            qksS[32 * t2:32 * t2 + 8,
                                 1024 * p2 + 256 * j:1024 * p2 + 256 * j + 128],
                            qksS[32 * t2:32 * t2 + 8,
                                 1024 * p2 + 256 * j + 128:1024 * p2 + 256 * j + 256],
                            start=True, stop=True, tile_position=(32 * t2, 0))
                return last
            kb.op("pe", sc8_fn, reads=[b_qksS[p2]],
                  writes=[b_scdA[p2], b_scdB[p2]])
            if P2N < 4:
                return
            kb.op("act", lambda e, p2=p2: e.activation(
                expTS[:, 1024 * p2:1024 * (p2 + 1)],
                A(scd[p2], 0, (1024, 128), (512, 2), (128, 4), (1, 128)),
                AF.Exp, scale=RSCALE),
                reads=[b_scdA[p2], b_scdB[p2]], writes=[b_expTS[p2]])

        def p2_av(sg, t0):
            p2 = sg % 2
            if P2N < 5:
                return
            def av8_fn(e, sg=sg, t0=t0, p2=p2):
                last = None
                for j in range(4):
                    for t2 in range(2):
                        tok = t0 + 8 * sg + 2 * j + t2
                        last = e.matmul(
                            A(pb[p2], 9 * (2 * j + t2), (512, 128), (1, 9)),
                            expTS[:, 1024 * p2 + 512 * t2 + 128 * j:
                                  1024 * p2 + 512 * t2 + 128 * j + 128],
                            A(vsl, tok, (9 * T, 128), (T, 9)),
                            start=True, stop=True)
                return last
            kb.op("pe", av8_fn, reads=[b_expTS[p2], b_vsl], writes=[b_pb[p2]])
            if P2N < 6:
                return
            tb = t0 + 8 * sg
            kb.op("dve", lambda e, p2=p2: e.reciprocal(
                rc[:, 8 * p2:8 * p2 + 8], A(pb[p2], 8, (512, 128), (9, 8))),
                reads=[b_pb[p2]], writes=[b_rc[p2]])
            kb.op("dve", lambda e, p2=p2, tb=tb: e.tensor_tensor(
                A(ssl, tb, (KV * T, 128), (1, 8), (T, 8)),
                A(pb[p2], 0, (512, 128), (9, 8), (1, 8)),
                A(rc, 8 * p2, (16, 128), (1, 8), (0, 8)),
                ALU.mult),
                reads=[b_pb[p2], b_rc[p2]], writes=[b_ssl])

        for sg in range(16):
            p2_tp(sg)
            if sg >= 1:
                p2_sc(sg - 1)
            if sg >= 2:
                p2_av(sg - 2, t0)
        p2_sc(15)
        p2_av(14, t0)
        p2_av(15, t0)

    ffnb = [[(pb[0], 0, 512, b_pb[0]), (pb[1], 0, 512, b_pb[1])],
            [(scd[0], 512, 1024, b_scdB[0]), (scd[1], 512, 1024, b_scdB[1])]]

    def p3_epilogue(c):
        cp = c % 2
        for half in range(NH):
            bt_, off_, pp_, bb_ = ffnb[cp][half]
            kb.op("act", lambda e, half=half, bt_=bt_, off_=off_, pp_=pp_: e.activation(
                A(gsb, 512 * half, (1024, 128), (1, HW)),
                A(bt_, off_, (pp_, 128), (1, HW)), AF.Silu),
                reads=[bb_], writes=[b_g[half]])
            kb.op("dve", lambda e, half=half, c=c: e.tensor_tensor(
                A(sgs, 512 * half, (1024, 128), (1, HW)),
                A(gsb, 512 * half, (1024, 128), (1, HW)),
                A(ssl, (c // 4) * T + half * HW, (KV * T, 128), (1, HW)),
                ALU.mult),
                reads=[b_g[half], b_ssl], writes=[b_sg[half]])
            def tp_fn(e, half=half):
                last = None
                for i in range(HW // 128):
                    last = e.transpose(
                        A(scd[half], 128 * i, (1024, 128), (1, 128)),
                        A(sgs, 512 * half + 128 * i, (1024, 128), (1, 128)),
                        idf[:, :])
                return last
            kb.op("pe", tp_fn, reads=[b_sg[half], b_idf], writes=[b_scdA[half]])
            kb.op("dve", lambda e, half=half: e.tensor_copy(
                A(gsb, 512 * half, (1024, 128), (1, HW)),
                A(scd[half], 0, (1024, 128), (1, HW))),
                reads=[b_scdA[half]], writes=[b_sge[half]])
            kb.op("pool", lambda e, c=c, half=half: e.dma_start(
                out=d_sg[half * HW:(half + 1) * HW, 128 * c:128 * (c + 1)].rearrange(
                    "(q p) d -> p q d", p=128),
                in_=A(gsb, 512 * half, (1024, 128), (128, HW // 128), (1, 128))),
                reads=[b_sge[half]], writes=[b_sgscr[c]], dma="sgscr")

    xts = [(xt, b_xt), (xtp, b_xtp)] + ([(xtv, b_xtv)] if xtv is not None else [])
    NXB = len(xts)

    def p4_xload(tt):
        xb, bxb = xts[tt % NXB]
        kb.op("pool", lambda e, tt=tt, xb=xb: e.dma_start(
            out=xb[:, :], in_=d_x[128 * tt:128 * (tt + 1), :]),
            writes=[bxb], dma=f"xta{tt % NXB}")

    for c in (range(OC) if "3" in phases else ()):
        par = pari % 2
        pari += 1
        cp = c % 2
        if c == 2 and "4" in phases:
            for t_ in range(min(NXB, NT)):
                p4_xload(t_)
        kb.op("sp", lambda e, c=c, par=par: e.dma_start(
            out=lhsT[par][:, :], in_=d_ccout[128 * c:128 * (c + 1), :]),
            reads=[b_ccout], writes=[b_lhs[par]], dma=f"lhss{par}")
        for half in range(NH):
            bt_, off_, pp_, bb_ = ffnb[cp][half]
            def ffn_fn(e, par=par, half=half, c=c, bt_=bt_, off_=off_, pp_=pp_):
                for k in range(NE):
                    e.matmul(A(bt_, off_, (pp_, 128), (1, HW)),
                             lhsT[par][:, 128 * k:128 * (k + 1)],
                             A(xnTh[half % NXH], k * TH, (NE * TH, 128), (1, HW)),
                             start=(k == 0), stop=False)
                return e.matmul(A(bt_, off_, (pp_, 128), (1, HW)),
                                Bt[:, 128 * c:128 * (c + 1)],
                                lm[:, half * HW:(half + 1) * HW],
                                start=False, stop=True)
            kb.op("pe", ffn_fn,
                  reads=[b_lhs[par], b_xnTh[half % NXH], b_Bt, b_lm],
                  writes=[bb_])
        if c > 0:
            p3_epilogue(c - 1)
    if "3" in phases:
        p3_epilogue(OC - 1)

    for tt in (range(NT) if "4" in phases else ()):
        par = tt % 2
        kb.op("sp" if par == 0 else "act", lambda e, tt=tt, par=par: e.dma_start(
            out=sgT[par][:, :], in_=d_sg[128 * tt:128 * (tt + 1), :]),
            reads=b_sgscr, writes=[b_sgT[par]], dma=f"sgT{par}")
        xb, bxb = xts[tt % NXB]
        for half in range(2):
            op = (tt * 2 + half) % 2
            kb.op("dve", lambda e, par=par, half=half, op=op, xb=xb, bxb=bxb:
              e.tensor_tensor(
                A(outh[op], 0, (2048, 128), (32, 64), (1, 32)),
                A(sgT[par], 64 * half, (4096, 128), (1, 64), (128, 32)),
                A(xb, 2048 * half, (E, 128), (32, 64), (1, 32)),
                ALU.add),
                reads=[b_sgT[par], bxb], writes=[b_outh[op]])
            kb.op("sp" if op == 0 else "act",
                  lambda e, tt=tt, half=half, op=op: e.dma_start(
                out=d_y[128 * tt:128 * (tt + 1), 2048 * half:2048 * (half + 1)],
                in_=outh[op][:, :]),
                reads=[b_outh[op]], writes=[b_ydma[tt * 2 + half]],
                dma="y" if op == 0 else "y2")
        if tt + NXB < NT:
            p4_xload(tt + NXB)

    if "4" in phases:
        kb.op("sp", lambda e: e.nop(), reads=b_ydma)
    else:
        kb.op("sp", lambda e: e.dma_start(out=d_y[0:128, 0:16], in_=bkv[:, :]),
              reads=[b_bkv], writes=[b_ydma[0]], dma="y")
        kb.op("sp", lambda e: e.nop(), reads=[b_ydma[0]])

    sems = {}
    semguards = []
    semkeys = [("c", x) for x in ENGS] + sorted(kb.dcount.keys(), key=str)
    with nc.allow_low_precision("bf16 matmul operands / bf16 stream by design"):
        for key in semkeys:
            g = nc.semaphore(f"s_{key[0]}_{key[1]}")
            semguards.append(g)
            sems[key] = g.__enter__()
        blk = nc.Block()
        block = blk.__enter__()
        kb.emit(block, sems)
        blk.__exit__(None, None, None)
    for g in reversed(semguards):
        g.__exit__(None, None, None)
    for g in reversed(guards):
        g.__exit__(None, None, None)
    return nc


def shard_inputs(L0_attn_x, L0_attn_Wq, L0_attn_bq, L0_attn_Wk, L0_attn_bk,
                 L0_attn_Wv, L0_attn_bv, L0_ffn_packed_weight, L0_ffn_block_scales,
                 L0_ffn_global_scale, L0_ffn_lora_A, L0_ffn_lora_B,
                 T=1024, ncores=8):
    x = np.ascontiguousarray(np.asarray(L0_attn_x, np.float32).reshape(-1, E))
    wq = np.asarray(L0_attn_Wq, np.float32).reshape(E, 128, 8, 4)
    wq_p = wq.transpose(2, 0, 1, 3).reshape(8, E, 512)
    wq_p = np.ascontiguousarray(
        wq_p.reshape(8, 8, 4, 128, 512).transpose(0, 1, 3, 2, 4).reshape(8, 8, 128, 2048))
    bq = np.ascontiguousarray(np.asarray(L0_attn_bq, np.float32))
    wk_p = np.asarray(L0_attn_Wk, np.float32).transpose(2, 0, 1)
    wk_p = np.ascontiguousarray(
        wk_p.reshape(8, 32, 128, 128).transpose(0, 2, 1, 3).reshape(8, 128, 4096))
    bk = np.ascontiguousarray(np.asarray(L0_attn_bk, np.float32))
    wv_p = np.asarray(L0_attn_Wv, np.float32).transpose(2, 0, 1)
    wv_p = np.ascontiguousarray(
        wv_p.reshape(8, 32, 128, 128).transpose(0, 2, 1, 3).reshape(8, 128, 4096))
    bv = np.ascontiguousarray(np.asarray(L0_attn_bv, np.float32))
    pk = np.asarray(L0_ffn_packed_weight, np.int32)
    sc = np.asarray(L0_ffn_block_scales, np.float32)
    gs = np.asarray(L0_ffn_global_scale, np.float32).reshape(1, 1)
    la = np.ascontiguousarray(np.asarray(L0_ffn_lora_A, np.float32))
    lb = np.asarray(L0_ffn_lora_B, np.float32)
    lb = np.ascontiguousarray(lb.reshape(32, 128, 16).transpose(1, 0, 2).reshape(128, 512))
    nh = 32 // ncores
    in_maps = []
    for r in range(ncores):
        idx = np.array([d * 32 + nh * r + h1 for h1 in range(nh) for d in range(128)])
        in_maps.append({
            "x_in": np.ascontiguousarray(x[r * T:(r + 1) * T]),
            "wq_p": wq_p, "bq_in": bq,
            "wk_p": wk_p, "bk_in": bk,
            "wv_p": wv_p, "bv_in": bv,
            "pk_in": np.ascontiguousarray(pk[idx]),
            "sc_in": np.ascontiguousarray(sc[idx]),
            "gs_in": gs, "a_in": la, "b_in": lb,
        })
    return in_maps


def kernel(**inputs):
    T, ncores = 1024, 8
    in_maps = shard_inputs(T=T, ncores=ncores, **inputs)
    nc = build_program(T, ncores)
    res = run_bass_kernel_spmd(nc, in_maps, core_ids=list(range(ncores)))
    y = np.concatenate([np.asarray(res.results[r]["y_out"], np.float32)
                        for r in range(ncores)], 0)
    return y.reshape(1, ncores * T, E)


# version 26
# speedup vs baseline: 1.0166x; 1.0166x over previous
import numpy as np
import ml_dtypes

import concourse.bass as bass
import concourse.mybir as mybir
from concourse.bass_utils import run_bass_kernel_spmd

F32 = mybir.dt.float32
BF16 = mybir.dt.bfloat16
I32 = mybir.dt.int32
AF = mybir.ActivationFunctionType
ALU = mybir.AluOpType
AX = mybir.AxisListType

E = 4096
NE = 32
KV = 8
OC = 32
RSCALE = float(1.0 / np.float32(np.sqrt(np.float32(128.0))))
P2N = 6

ENGS = ("sp", "act", "pe", "dve", "pool")


def A(t, offset, *dims):
    return bass.AP(t, offset, [list(d) for d in dims])


class Buf:
    __slots__ = ("name", "arena", "lo", "hi", "writer", "readers")

    def __init__(self, name, arena, lo, hi):
        self.name, self.arena, self.lo, self.hi = name, arena, lo, hi
        self.writer = None
        self.readers = []


class KB:

    def __init__(self):
        self.recs = []
        self.ccount = {e: 0 for e in ENGS}
        self.dcount = {}
        self.floors = {e: {} for e in ENGS}
        self.arenas = {}

    def buf(self, name, arena=None, lo=0, hi=1 << 30):
        b = Buf(name, arena if arena is not None else name, lo, hi)
        self.arenas.setdefault(b.arena, []).append(b)
        return b

    def _conf(self, b):
        for o in self.arenas[b.arena]:
            if o is not b and o.lo < b.hi and o.hi > b.lo:
                yield o

    def op(self, eng, fn, reads=(), writes=(), dma=False, cc=False):
        need = {}

        def req(dep):
            if dep and need.get(dep[0], 0) < dep[1]:
                need[dep[0]] = dep[1]

        for rb in reads:
            req(rb.writer)
            for o in self._conf(rb):
                req(o.writer)
        for wb in writes:
            req(wb.writer)
            for rd in wb.readers:
                req(rd)
            for o in self._conf(wb):
                req(o.writer)
                for rd in o.readers:
                    req(rd)
        waits = []
        fl = self.floors[eng]
        for sem, val in sorted(need.items(), key=lambda kv: str(kv[0])):
            if fl.get(sem, 0) >= val:
                continue
            if sem == ("c", eng) and val > self.ccount[eng]:
                raise RuntimeError(f"future dep on own engine {eng}")
            waits.append((sem, val))
            fl[sem] = val
        if cc:
            key = ("cc", eng)
            self.dcount[key] = self.dcount.get(key, 0) + 1
            mykey = (key, self.dcount[key])
            incs = [(key, 1)]
        elif dma:
            key = ("d", dma)
            self.dcount[key] = self.dcount.get(key, 0) + 16
            mykey = (key, self.dcount[key])
            incs = [(key, 16)]
        else:
            self.ccount[eng] += 1
            mykey = (("c", eng), self.ccount[eng])
            incs = [(("c", eng), 1)]
        self.recs.append((eng, fn, waits, incs))
        for wb in writes:
            wb.writer = mykey
            wb.readers = []
        for rb in reads:
            rb.readers.append(mykey)
        return mykey

    def emit(self, block, sems):
        def section(e, name):
            for eng, fn, waits, incs in self.recs:
                if eng != name:
                    continue
                for semkey, val in waits:
                    e.wait_ge(sems[semkey], val)
                inst = fn(e)
                for semkey, amt in incs:
                    inst.then_inc(sems[semkey], amt)

        @block.sync
        def _(e):
            section(e, "sp")

        @block.vector
        def _(e):
            section(e, "dve")

        @block.scalar
        def _(e):
            section(e, "act")

        @block.tensor
        def _(e):
            section(e, "pe")

        @block.gpsimd
        def _(e):
            section(e, "pool")


def build_program(T=1024, ncores=8, phases="abcd12345"):
    NT = T // 128
    HW = min(512, T)
    NH = T // HW
    NJT = OC // ncores
    NXH = 2 if T >= 1024 else 1
    TH = T // NXH
    nc = bass.Bass("TRN2", target_bir_lowering=False)

    d_x = nc.dram_tensor("x_in", [T, E], F32, kind="ExternalInput").ap()
    d_wq = nc.dram_tensor("wq_p", [KV, 8, 128, 2048], F32, kind="ExternalInput").ap()
    d_bq = nc.dram_tensor("bq_in", [128, 32], F32, kind="ExternalInput").ap()
    d_wk = nc.dram_tensor("wk_p", [KV, 128, 4096], F32, kind="ExternalInput").ap()
    d_bk = nc.dram_tensor("bk_in", [128, 8], F32, kind="ExternalInput").ap()
    d_wv = nc.dram_tensor("wv_p", [KV, 128, 4096], F32, kind="ExternalInput").ap()
    d_bv = nc.dram_tensor("bv_in", [128, 8], F32, kind="ExternalInput").ap()
    d_pk = nc.dram_tensor("pk_in", [128 * NJT, 2048], I32, kind="ExternalInput").ap()
    d_sc = nc.dram_tensor("sc_in", [128 * NJT, 256], F32, kind="ExternalInput").ap()
    d_gs = nc.dram_tensor("gs_in", [1, 1], F32, kind="ExternalInput").ap()
    d_a = nc.dram_tensor("a_in", [16, E], F32, kind="ExternalInput").ap()
    d_b = nc.dram_tensor("b_in", [128, 512], F32, kind="ExternalInput").ap()
    d_y = nc.dram_tensor("y_out", [T, E], F32, kind="ExternalOutput").ap()

    d_ccin = nc.dram_tensor("cc_in", [NJT * 128, 4096], BF16).ap()
    d_ccout = nc.dram_tensor("cc_out", [OC * 128, 4096], BF16).ap()
    d_sg = nc.dram_tensor("sg_scratch", [T, OC * 128], F32).ap()

    kb = KB()
    guards = []

    def sb(name, shape, dt):
        g = nc.sbuf_tensor(name, shape, dt)
        guards.append(g)
        return g.__enter__()

    def at(base_h, name, shape, dt, off):
        addr = nc.lookup_mloc(base_h).addr + off
        return nc.alloc_sbuf_tensor_at(name, shape, dt, offset=addr)

    def psum(name, shape, dt):
        g = nc.psum_tensor(name, shape, dt)
        guards.append(g)
        return g.__enter__()

    XB = max(NE * T * 2, 49152)
    xar = sb("xar", [128, XB], mybir.dt.uint8)
    xnTh = [at(xar, f"xnTh{i}", [128, NE * TH], BF16, i * NE * TH * 2) for i in range(NXH)]
    sgT = [at(xar, f"sgTt{i}", [128, E], F32, i * 16384) for i in range(2)]
    outh = [at(xar, f"outh{i}", [128, 2048], F32, 32768 + i * 8192) for i in range(2)]
    xt = sb("xt", [128, E], F32)
    xnb = sb("xnb", [128, E], BF16)
    qI = at(xt, "qI", [128, 4096], BF16, 0)
    kI = at(xt, "kI", [128, 4096], BF16, 8192)
    qsl = sb("qsl", [128, KV * T], BF16)
    if KV * T * 2 < 16384:
        sb("qslpad", [128, (16384 - KV * T * 2) // 2], BF16)
    xt2 = at(qsl, "xt2", [128, E], F32, 0)
    ksl = sb("ksl", [128, KV * T], BF16)
    if KV * T * 2 < 16384:
        sb("kslpad", [128, (16384 - KV * T * 2) // 2], BF16)
    vsl = sb("vsl", [128, 9 * T], BF16)
    if 9 * T * 2 < 8192:
        sb("vslpad", [128, (8192 - 9 * T * 2) // 2], BF16)
    junkb = at(vsl, "junkb", [128, E], BF16, 0)
    zar = sb("zar", [128, 9216], F32)
    ssl = at(zar, "ssl", [128, KV * T], F32, 0)
    pkt = at(zar, "pkt", [128, 2048], I32, 0)
    iscr = [at(zar, f"iscr{i}", [128, 512], I32, 8192 + i * 2048) for i in range(5)]
    wdec = at(zar, "wdec", [128, 4096], BF16, 18432)
    wtb = at(zar, "wtb", [128, 4096], BF16, 26624)
    lhsT = [sb("lhsA", [128, E], BF16), sb("lhsB", [128, E], BF16)]
    xtp = at(ksl, "xtp", [128, E], F32, 0)
    xtv = at(vsl, "xtv", [128, E], F32, 0) if 9 * T * 2 >= 16384 else None
    fold3 = [at(ksl, "fold3_0", [128, 2048], F32, 0),
             at(ksl, "fold3_1", [128, 2048], F32, 8192),
             at(vsl, "fold3_2", [128, 2048], F32, 0)]
    qksS = at(zar, "qksS", [64, 2048], BF16, 32768)
    expTS = sb("expTS", [128, 2048], BF16)
    rc = sb("rc", [128, 16], F32)
    lm = sb("lm", [16, T], BF16)
    At = sb("At", [128, 512], BF16)
    Bt = sb("Bt", [16, E], BF16)
    bqsb = sb("bqsb", [128, 32], F32)
    bqt = sb("bqt", [128, 8], F32)
    bkv = sb("bkv", [128, 16], F32)
    gbc = sb("gbc", [128, 1], F32)
    ones1 = sb("ones1", [1, 128], F32)
    scs = at(zar, "scs", [128, 256], F32, 34816)
    scs2 = at(zar, "scs2", [128, 256], F32, 35840)
    asb = at(xar, "asb", [16, E], F32, 0)
    bsb = at(xar, "bsb", [128, 512], F32, 16384)
    idb = sb("idb", [128, 128], BF16)
    idf = sb("idf", [128, 128], F32)
    gsb = at(qsl, "gsb", [128, 1024], F32, 0)
    sgs = at(qsl, "sgs", [128, 1024], F32, 4096)
    st = sb("st", [128, 16], F32)

    pb = [psum("pb0", [128, 512], F32), psum("pb1", [128, 512], F32),
          psum("pb2", [128, 1024], BF16), psum("pb3", [128, 1024], BF16)]
    scd = [psum("scd0", [128, 1024], F32), psum("scd1", [128, 1024], F32)]

    b_xnTh = [kb.buf(f"xnTh{i}", "X", i * NE * TH * 2, (i + 1) * NE * TH * 2)
              for i in range(NXH)]
    b_sgT = [kb.buf(f"sgT{i}", "X", i * 16384, (i + 1) * 16384) for i in range(2)]
    b_outh = [kb.buf(f"outh{i}", "X", 32768 + i * 8192, 32768 + (i + 1) * 8192)
              for i in range(2)]
    b_xt = kb.buf("xt", "XT", 0, 16384)
    b_xt2 = kb.buf("xt2", "QSL", 0, 16384)
    b_junk = kb.buf("junkb", "VSL", 0, 8192)
    b_xnb = kb.buf("xnb")
    b_qI = kb.buf("qI", "XT", 0, 8192)
    b_kI = kb.buf("kI", "XT", 8192, 16384)
    b_qsl = kb.buf("qsl", "QSL", 0, 16384)
    b_ksl = kb.buf("ksl", "KSL", 0, 16384)
    b_vsl = kb.buf("vsl", "VSL", 0, 18432)
    b_ssl = kb.buf("ssl", "Z", 0, 4 * KV * T)
    b_pk = kb.buf("pk", "Z", 0, 8192)
    b_int = [kb.buf(f"i{i}", "Z", 8192 + i * 2048, 8192 + (i + 1) * 2048) for i in range(5)]
    b_wdec = kb.buf("wdec", "Z", 18432, 26624)
    b_wtb = kb.buf("wtb", "Z", 26624, 34816)
    b_lhs = [kb.buf("lhsA"), kb.buf("lhsB")]
    b_xtp = kb.buf("xtp", "KSL", 0, 16384)
    b_xtv = kb.buf("xtv", "VSL", 0, 16384)
    b_fold = [kb.buf("fold0", "KSL", 0, 8192), kb.buf("fold1", "KSL", 8192, 16384),
              kb.buf("fold2", "VSL", 0, 8192)]
    b_qksS = [kb.buf(f"qksS{i}", "Z", 32768 + i * 2048, 32768 + (i + 1) * 2048)
              for i in range(2)]
    b_expTS = [kb.buf(f"expTS{i}", "expTS", i * 2048, (i + 1) * 2048) for i in range(2)]
    b_rc = [kb.buf(f"rc{i}", "rc", i * 32, (i + 1) * 32) for i in range(2)]
    b_lm = kb.buf("lm")
    b_At, b_Bt = kb.buf("At"), kb.buf("Bt")
    b_bqsb, b_bqt, b_bkv = kb.buf("bqsb"), kb.buf("bqt"), kb.buf("bkv")
    b_gbc = kb.buf("gbc")
    b_scs = kb.buf("scs", "Z", 34816, 35840)
    b_scs2 = kb.buf("scs2", "Z", 35840, 36864)
    b_ones1 = kb.buf("ones1")
    b_asb = kb.buf("asb", "X", 0, 16384)
    b_bsb = kb.buf("bsb", "X", 16384, 18432)
    b_idb, b_idf = kb.buf("idb"), kb.buf("idf")
    b_g = [kb.buf(f"g{i}", "QSL", i * 2048, (i + 1) * 2048) for i in range(2)]
    b_sge = [kb.buf(f"sge{i}", "QSL", i * 2048, (i + 1) * 2048) for i in range(2)]
    b_sg = [kb.buf(f"sg{i}", "QSL", 4096 + i * 2048, 4096 + (i + 1) * 2048)
            for i in range(2)]
    b_st2 = [kb.buf(f"st{i}", "ST", i * 32, (i + 1) * 32) for i in range(2)]
    b_pb = [kb.buf(f"pb{i}") for i in range(4)]
    b_scdA = [kb.buf(f"scd{i}A", f"SCD{i}", 0, 2048) for i in range(2)]
    b_scdB = [kb.buf(f"scd{i}B", f"SCD{i}", 2048, 4096) for i in range(2)]
    b_ccin = [kb.buf(f"ccin{j}", "CCIN", j, j + 1) for j in range(NJT)]
    b_ccout = kb.buf("ccout")
    b_sgscr = [kb.buf(f"sgscr{c}", "SGSCR", c, c + 1) for c in range(OC)]
    b_ydma = [kb.buf(f"y{i}", "YOUT", i, i + 1) for i in range(2 * NT)]

    XP = NE * T

    kb.op("pool", lambda e: e.memset(idb[:, :], 0.0), writes=[b_idb])
    kb.op("pool", lambda e: e.affine_select(
        out=idb[:, :], in_=idb[:, :], compare_op=ALU.not_equal, fill=1.0,
        base=0, pattern=[[-1, 128]], channel_multiplier=1),
        reads=[b_idb], writes=[b_idb])
    kb.op("pool", lambda e: e.memset(idf[:, :], 0.0), writes=[b_idf])
    kb.op("pool", lambda e: e.affine_select(
        out=idf[:, :], in_=idf[:, :], compare_op=ALU.not_equal, fill=1.0,
        base=0, pattern=[[-1, 128]], channel_multiplier=1),
        reads=[b_idf], writes=[b_idf])

    kb.op("sp", lambda e: e.dma_start(out=bqsb[:, :], in_=d_bq), writes=[b_bqsb], dma="bqsb")
    kb.op("dve", lambda e: e.tensor_reduce(
        bqt[:, :].rearrange("p (a b) -> p a b", b=1),
        bqsb[:, :].rearrange("p (a b) -> p a b", b=4), AX.X, ALU.add),
        reads=[b_bqsb], writes=[b_bqt])
    kb.op("sp", lambda e: e.dma_start(out=bkv[:, 0:8], in_=d_bk), writes=[b_bkv], dma="bkv")
    kb.op("sp", lambda e: e.dma_start(out=bkv[:, 8:16], in_=d_bv), writes=[b_bkv], dma="bkv")
    kb.op("sp", lambda e: e.dma_start(out=gbc[0:1, :], in_=d_gs), writes=[b_gbc], dma="gs")
    kb.op("pool", lambda e: e.memset(ones1[0:1, :], 1.0), writes=[b_ones1])
    kb.op("pe", lambda e: e.matmul(A(scd[1], 0, (1024, 128), (1, 1)), ones1[0:1, :],
                                   gbc[0:1, 0:1], start=True, stop=True),
          reads=[b_ones1, b_gbc], writes=[b_scdA[1]])
    kb.op("dve", lambda e: e.tensor_copy(gbc[:, :], A(scd[1], 0, (1024, 128), (1, 1))),
          reads=[b_scdA[1]], writes=[b_gbc])

    kb.op("sp", lambda e: e.dma_start(out=asb[:, :], in_=d_a), writes=[b_asb], dma="asb")
    for k in range(NE):
        kb.op("pe", lambda e, k=k: e.transpose(
            A(scd[1], 16 * k, (1024, 128), (1, 16)),
            asb[:, 128 * k:128 * (k + 1)], idf[0:16, 0:16]),
            reads=[b_asb, b_idf], writes=[b_scdA[1]])
    kb.op("dve", lambda e: e.tensor_copy(At[:, :], A(scd[1], 0, (1024, 128), (1, 512))),
          reads=[b_scdA[1]], writes=[b_At])

    kb.op("sp", lambda e: e.dma_start(out=bsb[:, :], in_=d_b),
          writes=[b_bsb], dma="bsb")
    for t in range(NE):
        kb.op("pe", lambda e, t=t: e.transpose(
            A(scd[1], 512 + 128 * (t % 4), (1024, 16), (1, 128)),
            bsb[:, 16 * t:16 * (t + 1)], idf[:, :]),
            reads=[b_bsb, b_idf], writes=[b_scdB[1]])
        if t % 4 == 3:
            def ev_b(e, t=t):
                last = None
                for u in range(t - 3, t + 1):
                    last = e.tensor_copy(
                        A(Bt, 4 * u, (E, 16), (1, 4), (128, 32)),
                        A(scd[1], 512 + 128 * (u % 4), (1024, 16), (32, 4), (1, 32)))
                return last
            kb.op("dve", ev_b, reads=[b_scdB[1]], writes=[b_Bt])

    def p0c_load(tt):
        xp = tt % 2
        xb, bxb = (xt, b_xt) if xp == 0 else (xt2, b_xt2)
        so = 8 * xp
        bst = b_st2[xp]
        kb.op("sp", lambda e, tt=tt, xb=xb: e.dma_start(
            out=xb[:, :], in_=d_x[128 * tt:128 * (tt + 1), :]),
            writes=[bxb], dma=f"xt{xp}")
        kb.op("act", lambda e, xb=xb, so=so: e.activation(
            junkb[:, :], xb[:, :], AF.Identity, accum_out=st[:, so:so + 1]),
            reads=[bxb], writes=[b_junk, bst])
        kb.op("act", lambda e, xb=xb, so=so: e.activation(
            junkb[:, :], xb[:, :], AF.Square, accum_out=st[:, so + 1:so + 2]),
            reads=[bxb], writes=[b_junk, bst])

    def p0c_finish(tt):
        xp = tt % 2
        xb, bxb = (xt, b_xt) if xp == 0 else (xt2, b_xt2)
        so = 8 * xp
        b_st = b_st2[xp]
        kb.op("dve", lambda e, so=so: e.tensor_scalar(
            st[:, so + 2:so + 3], st[:, so:so + 1], 1.0 / E, None, ALU.mult),
            reads=[b_st], writes=[b_st])
        kb.op("dve", lambda e, so=so: e.tensor_scalar(
            st[:, so + 3:so + 4], st[:, so + 2:so + 3], st[:, so + 2:so + 3],
            -1e-5, ALU.mult, ALU.add),
            reads=[b_st], writes=[b_st])
        kb.op("dve", lambda e, so=so: e.scalar_tensor_tensor(
            st[:, so + 4:so + 5], st[:, so + 1:so + 2], 1.0 / E,
            st[:, so + 3:so + 4], ALU.mult, ALU.subtract),
            reads=[b_st], writes=[b_st])
        kb.op("act", lambda e, so=so: e.activation(
            st[:, so + 5:so + 6], st[:, so + 4:so + 5], AF.Ln),
            reads=[b_st], writes=[b_st])
        kb.op("act", lambda e, so=so: e.activation(
            st[:, so + 6:so + 7], st[:, so + 5:so + 6], AF.Exp, scale=-0.5),
            reads=[b_st], writes=[b_st])
        kb.op("dve", lambda e, xb=xb, so=so: e.tensor_scalar(
            xnb[:, :], xb[:, :], st[:, so + 2:so + 3], st[:, so + 6:so + 7],
            ALU.subtract, ALU.mult),
            reads=[bxb, b_st], writes=[b_xnb])
        hh, tloc = (tt * NXH) // NT, tt % (NT // NXH)
        for k in range(NE):
            bi = 2 + (k // 8) % 2
            kb.op("pe", lambda e, k=k, bi=bi: e.transpose(
                A(pb[bi], 128 * (k % 8), (1024, 128), (1, 128)),
                xnb[:, 128 * k:128 * (k + 1)], idb[:, :]),
                reads=[b_xnb, b_idb], writes=[b_pb[bi]])
            if k % 8 == 7:
                kb.op("dve", lambda e, k=k, bi=bi, hh=hh, tloc=tloc: e.tensor_copy(
                    A(xnTh[hh], (k - 7) * TH + tloc * 128, (NE * TH, 128), (TH, 8), (1, 128)),
                    pb[bi][:, :]),
                    reads=[b_pb[bi]], writes=[b_xnTh[hh]])

    if "c" in phases:
        prev = None
        for tt in range(NT):
            p0c_load(tt)
            if prev is not None:
                p0c_finish(prev)
            prev = tt
        p0c_finish(prev)

    dec_pieces = []

    def _dec_load(jt):
        kb.op("pool", lambda e, jt=jt: e.dma_start(
            out=pkt[:, :], in_=d_pk[128 * jt:128 * (jt + 1), :]),
            reads=[b_ssl], writes=[b_pk], dma="pk")
        kb.op("pool", lambda e, jt=jt: e.dma_start(
            out=scs[:, :], in_=d_sc[128 * jt:128 * (jt + 1), :]),
            writes=[b_scs], dma="scs")
        kb.op("dve", lambda e: e.tensor_scalar(
            scs2[:, :], scs[:, :], gbc[:, 0:1], None, ALU.mult),
            reads=[b_scs, b_gbc], writes=[b_scs2])

    def _dec_sub(jt, s4):
        for ai, off in ((0, 0), (1, 1)):
            ic, isb = (0, 1) if ai == 0 else (3, 4)
            pks = pkt[:, 512 * s4:512 * (s4 + 1)]
            if ai == 0:
                kb.op("dve", lambda e, pks=pks, ic=ic: e.tensor_scalar(
                    iscr[ic][:, :], pks, 7, None, ALU.bitwise_and),
                    reads=[b_pk], writes=[b_int[ic]])
                kb.op("dve", lambda e, pks=pks, isb=isb: e.tensor_scalar(
                    iscr[isb][:, :], pks, 8, 28,
                    ALU.bitwise_and, ALU.logical_shift_left),
                    reads=[b_pk], writes=[b_int[isb]])
            else:
                kb.op("dve", lambda e, pks=pks, ic=ic: e.tensor_scalar(
                    iscr[ic][:, :], pks, 4, 7,
                    ALU.logical_shift_right, ALU.bitwise_and),
                    reads=[b_pk], writes=[b_int[ic]])
                kb.op("dve", lambda e, pks=pks, isb=isb: e.tensor_scalar(
                    iscr[isb][:, :], pks, 128, 24,
                    ALU.bitwise_and, ALU.logical_shift_left),
                    reads=[b_pk], writes=[b_int[isb]])
            kb.op("pool", lambda e, ic=ic: e.tensor_scalar(
                iscr[2][:, :], iscr[ic][:, :], 4194304, 1056964608,
                ALU.mult, ALU.add),
                reads=[b_int[ic]], writes=[b_int[2]])
            kb.op("dve", lambda e, ic=ic: e.scalar_tensor_tensor(
                iscr[2][:, :], iscr[ic][:, :], 1056964608, iscr[2][:, :],
                ALU.mult, ALU.min),
                reads=[b_int[ic], b_int[2]], writes=[b_int[2]])
            kb.op("dve", lambda e, ic=ic, isb=isb: e.tensor_tensor(
                iscr[ic][:, :], iscr[2][:, :], iscr[isb][:, :], ALU.bitwise_or),
                reads=[b_int[2], b_int[isb]], writes=[b_int[ic]])
            kb.op("pool", lambda e, s4=s4, off=off, ic=ic: e.tensor_tensor(
                A(wdec, 1024 * s4 + off, (4096, 128), (16, 64), (2, 8)),
                A(iscr[ic], 0, (512, 128), (8, 64), (1, 8)).bitcast(F32),
                A(scs2, 64 * s4, (256, 128), (1, 64), (0, 8)),
                ALU.mult),
                reads=[b_int[ic], b_scs2], writes=[b_wdec])

    def _dec_fin(jt):
        for k in range(NE):
            bi = 2 + (k // 8) % 2
            kb.op("pe", lambda e, k=k, bi=bi: e.transpose(
                A(pb[bi], 128 * (k % 8), (1024, 128), (1, 128)),
                wdec[:, 128 * k:128 * (k + 1)], idb[:, :]),
                reads=[b_wdec, b_idb], writes=[b_pb[bi]])
            if k % 8 == 7:
                kb.op("dve", lambda e, k=k, bi=bi: e.tensor_copy(
                    wtb[:, 128 * (k - 7):128 * (k + 1)], pb[bi][:, :]),
                    reads=[b_pb[bi]], writes=[b_wtb])
        kb.op("pool", lambda e, jt=jt: e.dma_start(
            out=d_ccin[128 * jt:128 * (jt + 1), :], in_=wtb[:, :]),
            reads=[b_wtb], writes=[b_ccin[jt]], dma="ccin")

    if "b" in phases:
        for jt in range(NJT):
            dec_pieces.append(lambda jt=jt: _dec_load(jt))
            for s4 in range(4):
                dec_pieces.append(lambda jt=jt, s4=s4: _dec_sub(jt, s4))
            dec_pieces.append(lambda jt=jt: _dec_fin(jt))

    pari = 0
    for kind in (("q", "k", "v") if "1" in phases else ()):
        for kv in range(KV):
            par = pari % 2
            pari += 1
            if kind == "q":
                for kq in range(8):
                    fp = kq % 3
                    kb.op("sp", lambda e, kv=kv, kq=kq, fp=fp: e.dma_start(
                        out=A(fold3[fp], 0, (2048, 128), (1, 2048)),
                        in_=d_wq[kv, kq]),
                        writes=[b_fold[fp]], dma=f"fold{fp}")
                    kb.op("dve", lambda e, kq=kq, par=par, fp=fp: e.tensor_reduce(
                        A(lhsT[par], 512 * kq, (E, 128), (128, 4), (1, 128), (1, 1)),
                        A(fold3[fp], 0, (2048, 128), (512, 4), (4, 128), (1, 4)),
                        AX.X, ALU.add),
                        reads=[b_fold[fp]], writes=[b_lhs[par]])
            else:
                if kind == "v" and kv == 0:
                    kb.op("pool", lambda e: e.memset(vsl[:, 8 * T:9 * T], 1.0),
                          writes=[b_vsl])
                src_ = d_wk if kind == "k" else d_wv
                kb.op("pool", lambda e, kv=kv, par=par, src_=src_: e.dma_start(
                    out=lhsT[par][:, :], in_=src_[kv]),
                    writes=[b_lhs[par]], dma=f"lhsp{par}")
            if dec_pieces:
                dec_pieces.pop(0)()
            for half in range(NH):
                def pj_fn(e, par=par, half=half):
                    last = None
                    for k in range(NE):
                        last = e.matmul(
                            A(pb[half], 0, (512, 128), (1, HW)),
                            lhsT[par][:, 128 * k:128 * (k + 1)],
                            A(xnTh[half % NXH], k * TH, (NE * TH, 128), (1, HW)),
                            start=(k == 0), stop=(k == NE - 1))
                    return last
                kb.op("pe", pj_fn, reads=[b_lhs[par], b_xnTh[half % NXH]],
                      writes=[b_pb[half]])
                if kind == "q":
                    dst, bias, bb, brd = qsl, bqt[:, kv:kv + 1], b_qsl, b_bqt
                elif kind == "k":
                    dst, bias, bb, brd = ksl, bkv[:, kv:kv + 1], b_ksl, b_bkv
                else:
                    dst, bias, bb, brd = vsl, bkv[:, 8 + kv:9 + kv], b_vsl, b_bkv
                kb.op("act", lambda e, kv=kv, half=half, dst=dst, bias=bias: e.activation(
                    dst[:, kv * T + half * HW: kv * T + (half + 1) * HW],
                    A(pb[half], 0, (512, 128), (1, HW)), AF.Identity, bias=bias),
                    reads=[b_pb[half], brd], writes=[bb])

    for half in (range(NH) if "1" in phases else ()):
        def lm_fn(e, half=half):
            last = None
            for k in range(NE):
                last = e.matmul(
                    A(scd[half], 0, (1024, 16), (1, HW)),
                    At[:, 16 * k:16 * (k + 1)],
                    A(xnTh[half % NXH], k * TH, (NE * TH, 128), (1, HW)),
                    start=(k == 0), stop=(k == NE - 1))
            return last
        kb.op("pe", lm_fn, reads=[b_At, b_xnTh[half % NXH]],
              writes=[b_scdA[half]])
        kb.op("act", lambda e, half=half: e.copy(
            lm[:, half * HW:(half + 1) * HW], A(scd[half], 0, (1024, 16), (1, HW))),
            reads=[b_scdA[half]], writes=[b_lm])

    while dec_pieces:
        dec_pieces.pop(0)()
    if "b" in phases:
        kb.op("pool", lambda e: e.collective_compute(
            "AllGather", ALU.bypass, replica_groups=[list(range(ncores))],
            ins=[d_ccin.opt()], outs=[d_ccout.opt()]),
            reads=b_ccin, writes=[b_ccout], cc=True)

    if "2" in phases:
        kb.op("pool", lambda e: e.memset(qI[:, :], 0.0), writes=[b_qI])
        kb.op("pool", lambda e: e.memset(kI[:, :], 0.0), writes=[b_kI])
    for ccu in (range(NT) if "2" in phases else ()):
        t0 = ccu * 128
        kb.op("dve", lambda e, t0=t0: e.tensor_copy(
            A(qI, 0, (4096, 128), (32, 128), (1, 8)),
            A(qsl, t0, (KV * T, 128), (1, 128), (T, 8))),
            reads=[b_qsl], writes=[b_qI])
        kb.op("dve", lambda e, t0=t0: e.tensor_copy(
            A(kI, 0, (4096, 128), (32, 128), (1, 8)),
            A(ksl, t0, (KV * T, 128), (1, 128), (T, 8))),
            reads=[b_ksl], writes=[b_kI])
        def p2_tp(sg):
            p2 = sg % 2
            bi = 2 + p2
            def tp8_fn(e, sg=sg, bi=bi):
                last = None
                for j in range(4):
                    g = 4 * sg + j
                    e.transpose(A(pb[bi], 256 * j, (1024, 64), (1, 128)),
                                kI[:, 64 * g:64 * g + 64], idb[:, :])
                    last = e.transpose(A(pb[bi], 256 * j + 128, (1024, 64), (1, 128)),
                                       qI[:, 64 * g:64 * g + 64], idb[:, :])
                return last
            kb.op("pe", tp8_fn, reads=[b_kI, b_qI, b_idb], writes=[b_pb[bi]])
            if P2N < 2:
                return
            kb.op("dve", lambda e, p2=p2, bi=bi: e.tensor_copy(
                qksS[0:64, 1024 * p2:1024 * (p2 + 1)],
                A(pb[bi], 0, (1024, 64), (1, 1024))),
                reads=[b_pb[bi]], writes=[b_qksS[p2]])

        def p2_sc(sg):
            p2 = sg % 2
            if P2N < 3:
                return
            def sc8_fn(e, p2=p2):
                last = None
                for j in range(4):
                    for t2 in range(2):
                        last = e.matmul(
                            A(scd[p2], 512 * t2 + 128 * j, (1024, 128), (1, 128)),
                            qksS[32 * t2:32 * t2 + 8,
                                 1024 * p2 + 256 * j:1024 * p2 + 256 * j + 128],
                            qksS[32 * t2:32 * t2 + 8,
                                 1024 * p2 + 256 * j + 128:1024 * p2 + 256 * j + 256],
                            start=True, stop=True, tile_position=(32 * t2, 0))
                return last
            kb.op("pe", sc8_fn, reads=[b_qksS[p2]],
                  writes=[b_scdA[p2], b_scdB[p2]])
            if P2N < 4:
                return
            kb.op("act", lambda e, p2=p2: e.activation(
                expTS[:, 1024 * p2:1024 * (p2 + 1)],
                A(scd[p2], 0, (1024, 128), (512, 2), (128, 4), (1, 128)),
                AF.Exp, scale=RSCALE),
                reads=[b_scdA[p2], b_scdB[p2]], writes=[b_expTS[p2]])

        def p2_av(sg, t0):
            p2 = sg % 2
            if P2N < 5:
                return
            def av8_fn(e, sg=sg, t0=t0, p2=p2):
                last = None
                for j in range(4):
                    for t2 in range(2):
                        tok = t0 + 8 * sg + 2 * j + t2
                        last = e.matmul(
                            A(pb[p2], 9 * (2 * j + t2), (512, 128), (1, 9)),
                            expTS[:, 1024 * p2 + 512 * t2 + 128 * j:
                                  1024 * p2 + 512 * t2 + 128 * j + 128],
                            A(vsl, tok, (9 * T, 128), (T, 9)),
                            start=True, stop=True)
                return last
            kb.op("pe", av8_fn, reads=[b_expTS[p2], b_vsl], writes=[b_pb[p2]])
            if P2N < 6:
                return
            tb = t0 + 8 * sg
            kb.op("dve", lambda e, p2=p2: e.reciprocal(
                rc[:, 8 * p2:8 * p2 + 8], A(pb[p2], 8, (512, 128), (9, 8))),
                reads=[b_pb[p2]], writes=[b_rc[p2]])
            kb.op("dve", lambda e, p2=p2, tb=tb: e.tensor_tensor(
                A(ssl, tb, (KV * T, 128), (1, 8), (T, 8)),
                A(pb[p2], 0, (512, 128), (9, 8), (1, 8)),
                A(rc, 8 * p2, (16, 128), (1, 8), (0, 8)),
                ALU.mult),
                reads=[b_pb[p2], b_rc[p2]], writes=[b_ssl])

        for sg in range(16):
            p2_tp(sg)
            if sg >= 1:
                p2_sc(sg - 1)
            if sg >= 2:
                p2_av(sg - 2, t0)
        p2_sc(15)
        p2_av(14, t0)
        p2_av(15, t0)

    ffnb = [[(pb[0], 0, 512, b_pb[0]), (pb[1], 0, 512, b_pb[1])],
            [(scd[0], 512, 1024, b_scdB[0]), (scd[1], 512, 1024, b_scdB[1])]]

    def p3_epilogue(c):
        cp = c % 2
        for half in range(NH):
            bt_, off_, pp_, bb_ = ffnb[cp][half]
            kb.op("act", lambda e, half=half, bt_=bt_, off_=off_, pp_=pp_: e.activation(
                A(gsb, 512 * half, (1024, 128), (1, HW)),
                A(bt_, off_, (pp_, 128), (1, HW)), AF.Silu),
                reads=[bb_], writes=[b_g[half]])
            kb.op("dve", lambda e, half=half, c=c: e.tensor_tensor(
                A(sgs, 512 * half, (1024, 128), (1, HW)),
                A(gsb, 512 * half, (1024, 128), (1, HW)),
                A(ssl, (c // 4) * T + half * HW, (KV * T, 128), (1, HW)),
                ALU.mult),
                reads=[b_g[half], b_ssl], writes=[b_sg[half]])
            def tp_fn(e, half=half):
                last = None
                for i in range(HW // 128):
                    last = e.transpose(
                        A(scd[half], 128 * i, (1024, 128), (1, 128)),
                        A(sgs, 512 * half + 128 * i, (1024, 128), (1, 128)),
                        idf[:, :])
                return last
            kb.op("pe", tp_fn, reads=[b_sg[half], b_idf], writes=[b_scdA[half]])
            kb.op("dve", lambda e, half=half: e.tensor_copy(
                A(gsb, 512 * half, (1024, 128), (1, HW)),
                A(scd[half], 0, (1024, 128), (1, HW))),
                reads=[b_scdA[half]], writes=[b_sge[half]])
            kb.op("pool", lambda e, c=c, half=half: e.dma_start(
                out=d_sg[half * HW:(half + 1) * HW, 128 * c:128 * (c + 1)].rearrange(
                    "(q p) d -> p q d", p=128),
                in_=A(gsb, 512 * half, (1024, 128), (128, HW // 128), (1, 128))),
                reads=[b_sge[half]], writes=[b_sgscr[c]], dma="sgscr")

    xts = [(xt, b_xt), (xtp, b_xtp)] + ([(xtv, b_xtv)] if xtv is not None else [])
    NXB = len(xts)

    def p4_xload(tt):
        xb, bxb = xts[tt % NXB]
        eng = "act" if tt % 2 == 0 else "sp"
        kb.op(eng, lambda e, tt=tt, xb=xb: e.dma_start(
            out=xb[:, :], in_=d_x[128 * tt:128 * (tt + 1), :]),
            writes=[bxb], dma=f"xt{eng}")

    for c in (range(OC) if "3" in phases else ()):
        par = pari % 2
        pari += 1
        cp = c % 2
        if c == 2 and "4" in phases:
            for t_ in range(min(NXB, NT)):
                p4_xload(t_)
        kb.op("sp", lambda e, c=c, par=par: e.dma_start(
            out=lhsT[par][:, :], in_=d_ccout[128 * c:128 * (c + 1), :]),
            reads=[b_ccout], writes=[b_lhs[par]], dma=f"lhss{par}")
        for half in range(NH):
            bt_, off_, pp_, bb_ = ffnb[cp][half]
            def ffn_fn(e, par=par, half=half, c=c, bt_=bt_, off_=off_, pp_=pp_):
                for k in range(NE):
                    e.matmul(A(bt_, off_, (pp_, 128), (1, HW)),
                             lhsT[par][:, 128 * k:128 * (k + 1)],
                             A(xnTh[half % NXH], k * TH, (NE * TH, 128), (1, HW)),
                             start=(k == 0), stop=False)
                return e.matmul(A(bt_, off_, (pp_, 128), (1, HW)),
                                Bt[:, 128 * c:128 * (c + 1)],
                                lm[:, half * HW:(half + 1) * HW],
                                start=False, stop=True)
            kb.op("pe", ffn_fn,
                  reads=[b_lhs[par], b_xnTh[half % NXH], b_Bt, b_lm],
                  writes=[bb_])
        if c > 0:
            p3_epilogue(c - 1)
    if "3" in phases:
        p3_epilogue(OC - 1)

    for tt in (range(NT) if "4" in phases else ()):
        par = tt % 2
        kb.op("sp" if par == 0 else "act", lambda e, tt=tt, par=par: e.dma_start(
            out=sgT[par][:, :], in_=d_sg[128 * tt:128 * (tt + 1), :]),
            reads=b_sgscr, writes=[b_sgT[par]], dma=f"sgT{par}")
        xb, bxb = xts[tt % NXB]
        for half in range(2):
            op = (tt * 2 + half) % 2
            kb.op("dve", lambda e, par=par, half=half, op=op, xb=xb, bxb=bxb:
              e.tensor_tensor(
                A(outh[op], 0, (2048, 128), (32, 64), (1, 32)),
                A(sgT[par], 64 * half, (4096, 128), (1, 64), (128, 32)),
                A(xb, 2048 * half, (E, 128), (32, 64), (1, 32)),
                ALU.add),
                reads=[b_sgT[par], bxb], writes=[b_outh[op]])
            kb.op("pool",
                  lambda e, tt=tt, half=half, op=op: e.dma_start(
                out=d_y[128 * tt:128 * (tt + 1), 2048 * half:2048 * (half + 1)],
                in_=outh[op][:, :]),
                reads=[b_outh[op]], writes=[b_ydma[tt * 2 + half]],
                dma="y")
        if tt + NXB < NT:
            p4_xload(tt + NXB)

    if "4" in phases:
        kb.op("sp", lambda e: e.nop(), reads=b_ydma)
    else:
        kb.op("sp", lambda e: e.dma_start(out=d_y[0:128, 0:16], in_=bkv[:, :]),
              reads=[b_bkv], writes=[b_ydma[0]], dma="y")
        kb.op("sp", lambda e: e.nop(), reads=[b_ydma[0]])

    sems = {}
    semguards = []
    semkeys = [("c", x) for x in ENGS] + sorted(kb.dcount.keys(), key=str)
    with nc.allow_low_precision("bf16 matmul operands / bf16 stream by design"):
        for key in semkeys:
            g = nc.semaphore(f"s_{key[0]}_{key[1]}")
            semguards.append(g)
            sems[key] = g.__enter__()
        blk = nc.Block()
        block = blk.__enter__()
        kb.emit(block, sems)
        blk.__exit__(None, None, None)
    for g in reversed(semguards):
        g.__exit__(None, None, None)
    for g in reversed(guards):
        g.__exit__(None, None, None)
    return nc


def shard_inputs(L0_attn_x, L0_attn_Wq, L0_attn_bq, L0_attn_Wk, L0_attn_bk,
                 L0_attn_Wv, L0_attn_bv, L0_ffn_packed_weight, L0_ffn_block_scales,
                 L0_ffn_global_scale, L0_ffn_lora_A, L0_ffn_lora_B,
                 T=1024, ncores=8):
    x = np.ascontiguousarray(np.asarray(L0_attn_x, np.float32).reshape(-1, E))
    wq = np.asarray(L0_attn_Wq, np.float32).reshape(E, 128, 8, 4)
    wq_p = wq.transpose(2, 0, 1, 3).reshape(8, E, 512)
    wq_p = np.ascontiguousarray(
        wq_p.reshape(8, 8, 4, 128, 512).transpose(0, 1, 3, 2, 4).reshape(8, 8, 128, 2048))
    bq = np.ascontiguousarray(np.asarray(L0_attn_bq, np.float32))
    wk_p = np.asarray(L0_attn_Wk, np.float32).transpose(2, 0, 1)
    wk_p = np.ascontiguousarray(
        wk_p.reshape(8, 32, 128, 128).transpose(0, 2, 1, 3).reshape(8, 128, 4096))
    bk = np.ascontiguousarray(np.asarray(L0_attn_bk, np.float32))
    wv_p = np.asarray(L0_attn_Wv, np.float32).transpose(2, 0, 1)
    wv_p = np.ascontiguousarray(
        wv_p.reshape(8, 32, 128, 128).transpose(0, 2, 1, 3).reshape(8, 128, 4096))
    bv = np.ascontiguousarray(np.asarray(L0_attn_bv, np.float32))
    pk = np.asarray(L0_ffn_packed_weight, np.int32)
    sc = np.asarray(L0_ffn_block_scales, np.float32)
    gs = np.asarray(L0_ffn_global_scale, np.float32).reshape(1, 1)
    la = np.ascontiguousarray(np.asarray(L0_ffn_lora_A, np.float32))
    lb = np.asarray(L0_ffn_lora_B, np.float32)
    lb = np.ascontiguousarray(lb.reshape(32, 128, 16).transpose(1, 0, 2).reshape(128, 512))
    nh = 32 // ncores
    in_maps = []
    for r in range(ncores):
        idx = np.array([d * 32 + nh * r + h1 for h1 in range(nh) for d in range(128)])
        in_maps.append({
            "x_in": np.ascontiguousarray(x[r * T:(r + 1) * T]),
            "wq_p": wq_p, "bq_in": bq,
            "wk_p": wk_p, "bk_in": bk,
            "wv_p": wv_p, "bv_in": bv,
            "pk_in": np.ascontiguousarray(pk[idx]),
            "sc_in": np.ascontiguousarray(sc[idx]),
            "gs_in": gs, "a_in": la, "b_in": lb,
        })
    return in_maps


def kernel(**inputs):
    T, ncores = 1024, 8
    in_maps = shard_inputs(T=T, ncores=ncores, **inputs)
    nc = build_program(T, ncores)
    res = run_bass_kernel_spmd(nc, in_maps, core_ids=list(range(ncores)))
    y = np.concatenate([np.asarray(res.results[r]["y_out"], np.float32)
                        for r in range(ncores)], 0)
    return y.reshape(1, ncores * T, E)


# version 27 (final)
# speedup vs baseline: 1.0310x; 1.0142x over previous
import numpy as np
import ml_dtypes

import concourse.bass as bass
import concourse.mybir as mybir
from concourse.bass_utils import run_bass_kernel_spmd

F32 = mybir.dt.float32
BF16 = mybir.dt.bfloat16
I32 = mybir.dt.int32
AF = mybir.ActivationFunctionType
ALU = mybir.AluOpType
AX = mybir.AxisListType

E = 4096
NE = 32
KV = 8
OC = 32
RSCALE = float(1.0 / np.float32(np.sqrt(np.float32(128.0))))
P2N = 6

ENGS = ("sp", "act", "pe", "dve", "pool")


def A(t, offset, *dims):
    return bass.AP(t, offset, [list(d) for d in dims])


class Buf:
    __slots__ = ("name", "arena", "lo", "hi", "writer", "readers")

    def __init__(self, name, arena, lo, hi):
        self.name, self.arena, self.lo, self.hi = name, arena, lo, hi
        self.writer = None
        self.readers = []


class KB:

    def __init__(self):
        self.recs = []
        self.ccount = {e: 0 for e in ENGS}
        self.dcount = {}
        self.floors = {e: {} for e in ENGS}
        self.arenas = {}

    def buf(self, name, arena=None, lo=0, hi=1 << 30):
        b = Buf(name, arena if arena is not None else name, lo, hi)
        self.arenas.setdefault(b.arena, []).append(b)
        return b

    def _conf(self, b):
        for o in self.arenas[b.arena]:
            if o is not b and o.lo < b.hi and o.hi > b.lo:
                yield o

    def op(self, eng, fn, reads=(), writes=(), dma=False, cc=False):
        need = {}

        def req(dep):
            if dep and need.get(dep[0], 0) < dep[1]:
                need[dep[0]] = dep[1]

        for rb in reads:
            req(rb.writer)
            for o in self._conf(rb):
                req(o.writer)
        for wb in writes:
            req(wb.writer)
            for rd in wb.readers:
                req(rd)
            for o in self._conf(wb):
                req(o.writer)
                for rd in o.readers:
                    req(rd)
        waits = []
        fl = self.floors[eng]
        for sem, val in sorted(need.items(), key=lambda kv: str(kv[0])):
            if fl.get(sem, 0) >= val:
                continue
            if sem == ("c", eng) and val > self.ccount[eng]:
                raise RuntimeError(f"future dep on own engine {eng}")
            waits.append((sem, val))
            fl[sem] = val
        if cc:
            key = ("cc", eng)
            self.dcount[key] = self.dcount.get(key, 0) + 1
            mykey = (key, self.dcount[key])
            incs = [(key, 1)]
        elif dma:
            key = ("d", dma)
            self.dcount[key] = self.dcount.get(key, 0) + 16
            mykey = (key, self.dcount[key])
            incs = [(key, 16)]
        else:
            self.ccount[eng] += 1
            mykey = (("c", eng), self.ccount[eng])
            incs = [(("c", eng), 1)]
        self.recs.append((eng, fn, waits, incs))
        for wb in writes:
            wb.writer = mykey
            wb.readers = []
        for rb in reads:
            rb.readers.append(mykey)
        return mykey

    def emit(self, block, sems):
        def section(e, name):
            for eng, fn, waits, incs in self.recs:
                if eng != name:
                    continue
                for semkey, val in waits:
                    e.wait_ge(sems[semkey], val)
                inst = fn(e)
                for semkey, amt in incs:
                    inst.then_inc(sems[semkey], amt)

        @block.sync
        def _(e):
            section(e, "sp")

        @block.vector
        def _(e):
            section(e, "dve")

        @block.scalar
        def _(e):
            section(e, "act")

        @block.tensor
        def _(e):
            section(e, "pe")

        @block.gpsimd
        def _(e):
            section(e, "pool")


def build_program(T=1024, ncores=8, phases="abcd12345"):
    NT = T // 128
    HW = min(512, T)
    NH = T // HW
    NJT = OC // ncores
    NXH = 2 if T >= 1024 else 1
    TH = T // NXH
    nc = bass.Bass("TRN2", target_bir_lowering=False)

    d_x = nc.dram_tensor("x_in", [T, E], F32, kind="ExternalInput").ap()
    d_wq = nc.dram_tensor("wq_p", [KV, 8, 128, 2048], F32, kind="ExternalInput").ap()
    d_bq = nc.dram_tensor("bq_in", [128, 32], F32, kind="ExternalInput").ap()
    d_wk = nc.dram_tensor("wk_p", [KV, 128, 4096], F32, kind="ExternalInput").ap()
    d_bk = nc.dram_tensor("bk_in", [128, 8], F32, kind="ExternalInput").ap()
    d_wv = nc.dram_tensor("wv_p", [KV, 128, 4096], F32, kind="ExternalInput").ap()
    d_bv = nc.dram_tensor("bv_in", [128, 8], F32, kind="ExternalInput").ap()
    d_pk = nc.dram_tensor("pk_in", [128 * NJT, 2048], I32, kind="ExternalInput").ap()
    d_sc = nc.dram_tensor("sc_in", [128 * NJT, 256], F32, kind="ExternalInput").ap()
    d_gs = nc.dram_tensor("gs_in", [1, 1], F32, kind="ExternalInput").ap()
    d_a = nc.dram_tensor("a_in", [16, E], F32, kind="ExternalInput").ap()
    d_b = nc.dram_tensor("b_in", [128, 512], F32, kind="ExternalInput").ap()
    d_y = nc.dram_tensor("y_out", [T, E], F32, kind="ExternalOutput").ap()

    d_ccin = nc.dram_tensor("cc_in", [NJT * 128, 4096], BF16).ap()
    d_ccout = nc.dram_tensor("cc_out", [OC * 128, 4096], BF16).ap()
    d_sg = nc.dram_tensor("sg_scratch", [T, OC * 128], F32).ap()

    kb = KB()
    guards = []

    def sb(name, shape, dt):
        g = nc.sbuf_tensor(name, shape, dt)
        guards.append(g)
        return g.__enter__()

    def at(base_h, name, shape, dt, off):
        addr = nc.lookup_mloc(base_h).addr + off
        return nc.alloc_sbuf_tensor_at(name, shape, dt, offset=addr)

    def psum(name, shape, dt):
        g = nc.psum_tensor(name, shape, dt)
        guards.append(g)
        return g.__enter__()

    XB = max(NE * T * 2, 49152)
    xar = sb("xar", [128, XB], mybir.dt.uint8)
    xnTh = [at(xar, f"xnTh{i}", [128, NE * TH], BF16, i * NE * TH * 2) for i in range(NXH)]
    sgT = [at(xar, f"sgTt{i}", [128, E], F32, i * 16384) for i in range(2)]
    outh = [at(xar, f"outh{i}", [128, 2048], F32, 32768 + i * 8192) for i in range(2)]
    xt = sb("xt", [128, E], F32)
    xnb = sb("xnb", [128, E], BF16)
    qI = at(xt, "qI", [128, 4096], BF16, 0)
    kI = at(xt, "kI", [128, 4096], BF16, 8192)
    qsl = sb("qsl", [128, KV * T], BF16)
    if KV * T * 2 < 16384:
        sb("qslpad", [128, (16384 - KV * T * 2) // 2], BF16)
    xt2 = at(qsl, "xt2", [128, E], F32, 0)
    ksl = sb("ksl", [128, KV * T], BF16)
    if KV * T * 2 < 16384:
        sb("kslpad", [128, (16384 - KV * T * 2) // 2], BF16)
    vsl = sb("vsl", [128, 9 * T], BF16)
    if 9 * T * 2 < 8192:
        sb("vslpad", [128, (8192 - 9 * T * 2) // 2], BF16)
    junkb = at(vsl, "junkb", [128, E], BF16, 0)
    zar = sb("zar", [128, 9216], F32)
    ssl = at(zar, "ssl", [128, KV * T], F32, 0)
    pkt = at(zar, "pkt", [128, 2048], I32, 0)
    iscr = [at(zar, f"iscr{i}", [128, 512], I32, 8192 + i * 2048) for i in range(5)]
    wdec = at(zar, "wdec", [128, 4096], BF16, 18432)
    wtb = at(zar, "wtb", [128, 4096], BF16, 26624)
    lhsT = [sb("lhsA", [128, E], BF16), sb("lhsB", [128, E], BF16)]
    xtp = at(ksl, "xtp", [128, E], F32, 0)
    xtv = at(vsl, "xtv", [128, E], F32, 0) if 9 * T * 2 >= 16384 else None
    fold3 = [at(ksl, "fold3_0", [128, 2048], F32, 0),
             at(ksl, "fold3_1", [128, 2048], F32, 8192),
             at(vsl, "fold3_2", [128, 2048], F32, 0)]
    qksS = at(zar, "qksS", [64, 2048], BF16, 32768)
    expTS = sb("expTS", [128, 2048], BF16)
    rc = sb("rc", [128, 16], F32)
    lm = sb("lm", [16, T], BF16)
    At = sb("At", [128, 512], BF16)
    Bt = sb("Bt", [16, E], BF16)
    bqsb = sb("bqsb", [128, 32], F32)
    bqt = sb("bqt", [128, 8], F32)
    bkv = sb("bkv", [128, 16], F32)
    gbc = sb("gbc", [128, 1], F32)
    ones1 = sb("ones1", [1, 128], F32)
    scs = at(zar, "scs", [128, 256], F32, 34816)
    scs2 = at(zar, "scs2", [128, 256], F32, 35840)
    asb = at(xar, "asb", [16, E], F32, 0)
    bsb = at(xar, "bsb", [128, 512], F32, 16384)
    idb = sb("idb", [128, 128], BF16)
    idf = sb("idf", [128, 128], F32)
    gsb = at(qsl, "gsb", [128, 1024], F32, 0)
    sgs = at(qsl, "sgs", [128, 1024], F32, 4096)
    st = sb("st", [128, 16], F32)

    pb = [psum("pb0", [128, 512], F32), psum("pb1", [128, 512], F32),
          psum("pb2", [128, 1024], BF16), psum("pb3", [128, 1024], BF16)]
    scd = [psum("scd0", [128, 1024], F32), psum("scd1", [128, 1024], F32)]

    b_xnTh = [kb.buf(f"xnTh{i}", "X", i * NE * TH * 2, (i + 1) * NE * TH * 2)
              for i in range(NXH)]
    b_sgT = [kb.buf(f"sgT{i}", "X", i * 16384, (i + 1) * 16384) for i in range(2)]
    b_outh = [kb.buf(f"outh{i}", "X", 32768 + i * 8192, 32768 + (i + 1) * 8192)
              for i in range(2)]
    b_xt = kb.buf("xt", "XT", 0, 16384)
    b_xt2 = kb.buf("xt2", "QSL", 0, 16384)
    b_junk = kb.buf("junkb", "VSL", 0, 8192)
    b_xnb = kb.buf("xnb")
    b_qI = kb.buf("qI", "XT", 0, 8192)
    b_kI = kb.buf("kI", "XT", 8192, 16384)
    b_qsl = kb.buf("qsl", "QSL", 0, 16384)
    b_ksl = kb.buf("ksl", "KSL", 0, 16384)
    b_vsl = kb.buf("vsl", "VSL", 0, 18432)
    b_ssl = kb.buf("ssl", "Z", 0, 4 * KV * T)
    b_pk = kb.buf("pk", "Z", 0, 8192)
    b_int = [kb.buf(f"i{i}", "Z", 8192 + i * 2048, 8192 + (i + 1) * 2048) for i in range(5)]
    b_wdec = kb.buf("wdec", "Z", 18432, 26624)
    b_wtb = kb.buf("wtb", "Z", 26624, 34816)
    b_lhs = [kb.buf("lhsA"), kb.buf("lhsB")]
    b_xtp = kb.buf("xtp", "KSL", 0, 16384)
    b_xtv = kb.buf("xtv", "VSL", 0, 16384)
    b_fold = [kb.buf("fold0", "KSL", 0, 8192), kb.buf("fold1", "KSL", 8192, 16384),
              kb.buf("fold2", "VSL", 0, 8192)]
    b_qksS = [kb.buf(f"qksS{i}", "Z", 32768 + i * 2048, 32768 + (i + 1) * 2048)
              for i in range(2)]
    b_expTS = [kb.buf(f"expTS{i}", "expTS", i * 2048, (i + 1) * 2048) for i in range(2)]
    b_rc = [kb.buf(f"rc{i}", "rc", i * 32, (i + 1) * 32) for i in range(2)]
    b_lm = kb.buf("lm")
    b_At, b_Bt = kb.buf("At"), kb.buf("Bt")
    b_bqsb, b_bqt, b_bkv = kb.buf("bqsb"), kb.buf("bqt"), kb.buf("bkv")
    b_gbc = kb.buf("gbc")
    b_scs = kb.buf("scs", "Z", 34816, 35840)
    b_scs2 = kb.buf("scs2", "Z", 35840, 36864)
    b_ones1 = kb.buf("ones1")
    b_asb = kb.buf("asb", "X", 0, 16384)
    b_bsb = kb.buf("bsb", "X", 16384, 18432)
    b_idb, b_idf = kb.buf("idb"), kb.buf("idf")
    b_g = [kb.buf(f"g{i}", "QSL", i * 2048, (i + 1) * 2048) for i in range(2)]
    b_sge = [kb.buf(f"sge{i}", "QSL", i * 2048, (i + 1) * 2048) for i in range(2)]
    b_sg = [kb.buf(f"sg{i}", "QSL", 4096 + i * 2048, 4096 + (i + 1) * 2048)
            for i in range(2)]
    b_st2 = [kb.buf(f"st{i}", "ST", i * 32, (i + 1) * 32) for i in range(2)]
    b_pb = [kb.buf(f"pb{i}") for i in range(4)]
    b_scdA = [kb.buf(f"scd{i}A", f"SCD{i}", 0, 2048) for i in range(2)]
    b_scdB = [kb.buf(f"scd{i}B", f"SCD{i}", 2048, 4096) for i in range(2)]
    b_ccin = [kb.buf(f"ccin{j}", "CCIN", j, j + 1) for j in range(NJT)]
    b_ccout = kb.buf("ccout")
    b_sgscr = [kb.buf(f"sgscr{c}", "SGSCR", c, c + 1) for c in range(OC)]
    b_ydma = [kb.buf(f"y{i}", "YOUT", i, i + 1) for i in range(2 * NT)]

    XP = NE * T

    kb.op("pool", lambda e: e.memset(idb[:, :], 0.0), writes=[b_idb])
    kb.op("pool", lambda e: e.affine_select(
        out=idb[:, :], in_=idb[:, :], compare_op=ALU.not_equal, fill=1.0,
        base=0, pattern=[[-1, 128]], channel_multiplier=1),
        reads=[b_idb], writes=[b_idb])
    kb.op("pool", lambda e: e.memset(idf[:, :], 0.0), writes=[b_idf])
    kb.op("pool", lambda e: e.affine_select(
        out=idf[:, :], in_=idf[:, :], compare_op=ALU.not_equal, fill=1.0,
        base=0, pattern=[[-1, 128]], channel_multiplier=1),
        reads=[b_idf], writes=[b_idf])

    kb.op("sp", lambda e: e.dma_start(out=bqsb[:, :], in_=d_bq), writes=[b_bqsb], dma="bqsb")
    kb.op("dve", lambda e: e.tensor_reduce(
        bqt[:, :].rearrange("p (a b) -> p a b", b=1),
        bqsb[:, :].rearrange("p (a b) -> p a b", b=4), AX.X, ALU.add),
        reads=[b_bqsb], writes=[b_bqt])
    kb.op("sp", lambda e: e.dma_start(out=bkv[:, 0:8], in_=d_bk), writes=[b_bkv], dma="bkv")
    kb.op("sp", lambda e: e.dma_start(out=bkv[:, 8:16], in_=d_bv), writes=[b_bkv], dma="bkv")
    kb.op("sp", lambda e: e.dma_start(out=gbc[0:1, :], in_=d_gs), writes=[b_gbc], dma="gs")
    kb.op("pool", lambda e: e.memset(ones1[0:1, :], 1.0), writes=[b_ones1])
    kb.op("pe", lambda e: e.matmul(A(scd[1], 0, (1024, 128), (1, 1)), ones1[0:1, :],
                                   gbc[0:1, 0:1], start=True, stop=True),
          reads=[b_ones1, b_gbc], writes=[b_scdA[1]])
    kb.op("dve", lambda e: e.tensor_copy(gbc[:, :], A(scd[1], 0, (1024, 128), (1, 1))),
          reads=[b_scdA[1]], writes=[b_gbc])

    kb.op("sp", lambda e: e.dma_start(out=asb[:, :], in_=d_a), writes=[b_asb], dma="asb")
    for k in range(NE):
        kb.op("pe", lambda e, k=k: e.transpose(
            A(scd[1], 16 * k, (1024, 128), (1, 16)),
            asb[:, 128 * k:128 * (k + 1)], idf[0:16, 0:16]),
            reads=[b_asb, b_idf], writes=[b_scdA[1]])
    kb.op("dve", lambda e: e.tensor_copy(At[:, :], A(scd[1], 0, (1024, 128), (1, 512))),
          reads=[b_scdA[1]], writes=[b_At])

    kb.op("sp", lambda e: e.dma_start(out=bsb[:, :], in_=d_b),
          writes=[b_bsb], dma="bsb")
    for t in range(NE):
        kb.op("pe", lambda e, t=t: e.transpose(
            A(scd[1], 512 + 128 * (t % 4), (1024, 16), (1, 128)),
            bsb[:, 16 * t:16 * (t + 1)], idf[:, :]),
            reads=[b_bsb, b_idf], writes=[b_scdB[1]])
        if t % 4 == 3:
            def ev_b(e, t=t):
                last = None
                for u in range(t - 3, t + 1):
                    last = e.tensor_copy(
                        A(Bt, 4 * u, (E, 16), (1, 4), (128, 32)),
                        A(scd[1], 512 + 128 * (u % 4), (1024, 16), (32, 4), (1, 32)))
                return last
            kb.op("dve", ev_b, reads=[b_scdB[1]], writes=[b_Bt])

    def p0c_load(tt):
        xp = tt % 2
        xb, bxb = (xt, b_xt) if xp == 0 else (xt2, b_xt2)
        so = 8 * xp
        bst = b_st2[xp]
        kb.op("sp", lambda e, tt=tt, xb=xb: e.dma_start(
            out=xb[:, :], in_=d_x[128 * tt:128 * (tt + 1), :]),
            writes=[bxb], dma=f"xt{xp}")
        kb.op("act", lambda e, xb=xb, so=so: e.activation(
            junkb[:, :], xb[:, :], AF.Identity, accum_out=st[:, so:so + 1]),
            reads=[bxb], writes=[b_junk, bst])
        kb.op("act", lambda e, xb=xb, so=so: e.activation(
            junkb[:, :], xb[:, :], AF.Square, accum_out=st[:, so + 1:so + 2]),
            reads=[bxb], writes=[b_junk, bst])

    def p0c_finish(tt):
        xp = tt % 2
        xb, bxb = (xt, b_xt) if xp == 0 else (xt2, b_xt2)
        so = 8 * xp
        b_st = b_st2[xp]
        kb.op("dve", lambda e, so=so: e.tensor_scalar(
            st[:, so + 2:so + 3], st[:, so:so + 1], 1.0 / E, None, ALU.mult),
            reads=[b_st], writes=[b_st])
        kb.op("dve", lambda e, so=so: e.tensor_scalar(
            st[:, so + 3:so + 4], st[:, so + 2:so + 3], st[:, so + 2:so + 3],
            -1e-5, ALU.mult, ALU.add),
            reads=[b_st], writes=[b_st])
        kb.op("dve", lambda e, so=so: e.scalar_tensor_tensor(
            st[:, so + 4:so + 5], st[:, so + 1:so + 2], 1.0 / E,
            st[:, so + 3:so + 4], ALU.mult, ALU.subtract),
            reads=[b_st], writes=[b_st])
        kb.op("act", lambda e, so=so: e.activation(
            st[:, so + 5:so + 6], st[:, so + 4:so + 5], AF.Ln),
            reads=[b_st], writes=[b_st])
        kb.op("act", lambda e, so=so: e.activation(
            st[:, so + 6:so + 7], st[:, so + 5:so + 6], AF.Exp, scale=-0.5),
            reads=[b_st], writes=[b_st])
        kb.op("dve", lambda e, xb=xb, so=so: e.tensor_scalar(
            xnb[:, :], xb[:, :], st[:, so + 2:so + 3], st[:, so + 6:so + 7],
            ALU.subtract, ALU.mult),
            reads=[bxb, b_st], writes=[b_xnb])
        hh, tloc = (tt * NXH) // NT, tt % (NT // NXH)
        for k in range(NE):
            bi = 2 + (k // 8) % 2
            kb.op("pe", lambda e, k=k, bi=bi: e.transpose(
                A(pb[bi], 128 * (k % 8), (1024, 128), (1, 128)),
                xnb[:, 128 * k:128 * (k + 1)], idb[:, :]),
                reads=[b_xnb, b_idb], writes=[b_pb[bi]])
            if k % 8 == 7:
                kb.op("dve", lambda e, k=k, bi=bi, hh=hh, tloc=tloc: e.tensor_copy(
                    A(xnTh[hh], (k - 7) * TH + tloc * 128, (NE * TH, 128), (TH, 8), (1, 128)),
                    pb[bi][:, :]),
                    reads=[b_pb[bi]], writes=[b_xnTh[hh]])

    if "c" in phases:
        prev = None
        for tt in range(NT):
            p0c_load(tt)
            if prev is not None:
                p0c_finish(prev)
            prev = tt
        p0c_finish(prev)

    dec_pieces = []

    def _dec_load(jt):
        kb.op("pool", lambda e, jt=jt: e.dma_start(
            out=pkt[:, :], in_=d_pk[128 * jt:128 * (jt + 1), :]),
            reads=[b_ssl], writes=[b_pk], dma="pk")
        kb.op("pool", lambda e, jt=jt: e.dma_start(
            out=scs[:, :], in_=d_sc[128 * jt:128 * (jt + 1), :]),
            writes=[b_scs], dma="scs")
        kb.op("dve", lambda e: e.tensor_scalar(
            scs2[:, :], scs[:, :], gbc[:, 0:1], None, ALU.mult),
            reads=[b_scs, b_gbc], writes=[b_scs2])

    def _dec_sub(jt, s4):
        for ai, off in ((0, 0), (1, 1)):
            ic, isb = (0, 1) if ai == 0 else (3, 4)
            pks = pkt[:, 512 * s4:512 * (s4 + 1)]
            if ai == 0:
                kb.op("dve", lambda e, pks=pks, ic=ic: e.tensor_scalar(
                    iscr[ic][:, :], pks, 7, None, ALU.bitwise_and),
                    reads=[b_pk], writes=[b_int[ic]])
                kb.op("dve", lambda e, pks=pks, isb=isb: e.tensor_scalar(
                    iscr[isb][:, :], pks, 8, 28,
                    ALU.bitwise_and, ALU.logical_shift_left),
                    reads=[b_pk], writes=[b_int[isb]])
            else:
                kb.op("dve", lambda e, pks=pks, ic=ic: e.tensor_scalar(
                    iscr[ic][:, :], pks, 4, 7,
                    ALU.logical_shift_right, ALU.bitwise_and),
                    reads=[b_pk], writes=[b_int[ic]])
                kb.op("dve", lambda e, pks=pks, isb=isb: e.tensor_scalar(
                    iscr[isb][:, :], pks, 128, 24,
                    ALU.bitwise_and, ALU.logical_shift_left),
                    reads=[b_pk], writes=[b_int[isb]])
            kb.op("pool", lambda e, ic=ic: e.tensor_scalar(
                iscr[2][:, :], iscr[ic][:, :], 4194304, 1056964608,
                ALU.mult, ALU.add),
                reads=[b_int[ic]], writes=[b_int[2]])
            kb.op("dve", lambda e, ic=ic: e.scalar_tensor_tensor(
                iscr[2][:, :], iscr[ic][:, :], 1056964608, iscr[2][:, :],
                ALU.mult, ALU.min),
                reads=[b_int[ic], b_int[2]], writes=[b_int[2]])
            kb.op("dve", lambda e, ic=ic, isb=isb: e.tensor_tensor(
                iscr[ic][:, :], iscr[2][:, :], iscr[isb][:, :], ALU.bitwise_or),
                reads=[b_int[2], b_int[isb]], writes=[b_int[ic]])
            kb.op("pool", lambda e, s4=s4, off=off, ic=ic: e.tensor_tensor(
                A(wdec, 1024 * s4 + off, (4096, 128), (16, 64), (2, 8)),
                A(iscr[ic], 0, (512, 128), (8, 64), (1, 8)).bitcast(F32),
                A(scs2, 64 * s4, (256, 128), (1, 64), (0, 8)),
                ALU.mult),
                reads=[b_int[ic], b_scs2], writes=[b_wdec])

    def _dec_fin(jt):
        for k in range(NE):
            bi = 2 + (k // 8) % 2
            kb.op("pe", lambda e, k=k, bi=bi: e.transpose(
                A(pb[bi], 128 * (k % 8), (1024, 128), (1, 128)),
                wdec[:, 128 * k:128 * (k + 1)], idb[:, :]),
                reads=[b_wdec, b_idb], writes=[b_pb[bi]])
            if k % 8 == 7:
                kb.op("dve", lambda e, k=k, bi=bi: e.tensor_copy(
                    wtb[:, 128 * (k - 7):128 * (k + 1)], pb[bi][:, :]),
                    reads=[b_pb[bi]], writes=[b_wtb])
        kb.op("pool", lambda e, jt=jt: e.dma_start(
            out=d_ccin[128 * jt:128 * (jt + 1), :], in_=wtb[:, :]),
            reads=[b_wtb], writes=[b_ccin[jt]], dma="ccin")

    if "b" in phases:
        for jt in range(NJT):
            dec_pieces.append(lambda jt=jt: _dec_load(jt))
            for s4 in range(4):
                dec_pieces.append(lambda jt=jt, s4=s4: _dec_sub(jt, s4))
            dec_pieces.append(lambda jt=jt: _dec_fin(jt))

    pari = 0
    for kind in (("q", "k", "v") if "1" in phases else ()):
        for kv in range(KV):
            par = pari % 2
            pari += 1
            if kind == "q":
                for kq in range(8):
                    fp = kq % 3
                    kb.op("sp", lambda e, kv=kv, kq=kq, fp=fp: e.dma_start(
                        out=A(fold3[fp], 0, (2048, 128), (1, 2048)),
                        in_=d_wq[kv, kq]),
                        writes=[b_fold[fp]], dma=f"fold{fp}")
                    kb.op("dve", lambda e, kq=kq, par=par, fp=fp: e.tensor_reduce(
                        A(lhsT[par], 512 * kq, (E, 128), (128, 4), (1, 128), (1, 1)),
                        A(fold3[fp], 0, (2048, 128), (512, 4), (4, 128), (1, 4)),
                        AX.X, ALU.add),
                        reads=[b_fold[fp]], writes=[b_lhs[par]])
            else:
                if kind == "v" and kv == 0:
                    kb.op("pool", lambda e: e.memset(vsl[:, 8 * T:9 * T], 1.0),
                          writes=[b_vsl])
                src_ = d_wk if kind == "k" else d_wv
                kb.op("pool", lambda e, kv=kv, par=par, src_=src_: e.dma_start(
                    out=lhsT[par][:, :], in_=src_[kv]),
                    writes=[b_lhs[par]], dma=f"lhsp{par}")
            if dec_pieces:
                dec_pieces.pop(0)()
            for half in range(NH):
                def pj_fn(e, par=par, half=half):
                    last = None
                    for k in range(NE):
                        last = e.matmul(
                            A(pb[half], 0, (512, 128), (1, HW)),
                            lhsT[par][:, 128 * k:128 * (k + 1)],
                            A(xnTh[half % NXH], k * TH, (NE * TH, 128), (1, HW)),
                            start=(k == 0), stop=(k == NE - 1))
                    return last
                kb.op("pe", pj_fn, reads=[b_lhs[par], b_xnTh[half % NXH]],
                      writes=[b_pb[half]])
                if kind == "q":
                    dst, bias, bb, brd = qsl, bqt[:, kv:kv + 1], b_qsl, b_bqt
                elif kind == "k":
                    dst, bias, bb, brd = ksl, bkv[:, kv:kv + 1], b_ksl, b_bkv
                else:
                    dst, bias, bb, brd = vsl, bkv[:, 8 + kv:9 + kv], b_vsl, b_bkv
                kb.op("act", lambda e, kv=kv, half=half, dst=dst, bias=bias: e.activation(
                    dst[:, kv * T + half * HW: kv * T + (half + 1) * HW],
                    A(pb[half], 0, (512, 128), (1, HW)), AF.Identity, bias=bias),
                    reads=[b_pb[half], brd], writes=[bb])

    for half in (range(NH) if "1" in phases else ()):
        def lm_fn(e, half=half):
            last = None
            for k in range(NE):
                last = e.matmul(
                    A(scd[half], 0, (1024, 16), (1, HW)),
                    At[:, 16 * k:16 * (k + 1)],
                    A(xnTh[half % NXH], k * TH, (NE * TH, 128), (1, HW)),
                    start=(k == 0), stop=(k == NE - 1))
            return last
        kb.op("pe", lm_fn, reads=[b_At, b_xnTh[half % NXH]],
              writes=[b_scdA[half]])
        kb.op("act", lambda e, half=half: e.copy(
            lm[:, half * HW:(half + 1) * HW], A(scd[half], 0, (1024, 16), (1, HW))),
            reads=[b_scdA[half]], writes=[b_lm])

    while dec_pieces:
        dec_pieces.pop(0)()
    if "b" in phases:
        kb.op("pool", lambda e: e.collective_compute(
            "AllGather", ALU.bypass, replica_groups=[list(range(ncores))],
            ins=[d_ccin.opt()], outs=[d_ccout.opt()]),
            reads=b_ccin, writes=[b_ccout], cc=True)

    if "2" in phases:
        kb.op("pool", lambda e: e.memset(qI[:, :], 0.0), writes=[b_qI])
        kb.op("pool", lambda e: e.memset(kI[:, :], 0.0), writes=[b_kI])
    for ccu in (range(NT) if "2" in phases else ()):
        t0 = ccu * 128
        kb.op("dve", lambda e, t0=t0: e.tensor_copy(
            A(qI, 0, (4096, 128), (32, 128), (1, 8)),
            A(qsl, t0, (KV * T, 128), (1, 128), (T, 8))),
            reads=[b_qsl], writes=[b_qI])
        kb.op("dve", lambda e, t0=t0: e.tensor_copy(
            A(kI, 0, (4096, 128), (32, 128), (1, 8)),
            A(ksl, t0, (KV * T, 128), (1, 128), (T, 8))),
            reads=[b_ksl], writes=[b_kI])
        def p2_tp(sg):
            p2 = sg % 2
            bi = 2 + p2
            def tp8_fn(e, sg=sg, bi=bi):
                last = None
                for j in range(4):
                    g = 4 * sg + j
                    e.transpose(A(pb[bi], 256 * j, (1024, 64), (1, 128)),
                                kI[:, 64 * g:64 * g + 64], idb[:, :])
                    last = e.transpose(A(pb[bi], 256 * j + 128, (1024, 64), (1, 128)),
                                       qI[:, 64 * g:64 * g + 64], idb[:, :])
                return last
            kb.op("pe", tp8_fn, reads=[b_kI, b_qI, b_idb], writes=[b_pb[bi]])
            if P2N < 2:
                return
            kb.op("dve", lambda e, p2=p2, bi=bi: e.tensor_copy(
                qksS[0:64, 1024 * p2:1024 * (p2 + 1)],
                A(pb[bi], 0, (1024, 64), (1, 1024))),
                reads=[b_pb[bi]], writes=[b_qksS[p2]])

        def p2_sc(sg):
            p2 = sg % 2
            if P2N < 3:
                return
            def sc8_fn(e, p2=p2):
                last = None
                for j in range(4):
                    for t2 in range(2):
                        last = e.matmul(
                            A(scd[p2], 512 * t2 + 128 * j, (1024, 128), (1, 128)),
                            qksS[32 * t2:32 * t2 + 8,
                                 1024 * p2 + 256 * j:1024 * p2 + 256 * j + 128],
                            qksS[32 * t2:32 * t2 + 8,
                                 1024 * p2 + 256 * j + 128:1024 * p2 + 256 * j + 256],
                            start=True, stop=True, tile_position=(32 * t2, 0))
                return last
            kb.op("pe", sc8_fn, reads=[b_qksS[p2]],
                  writes=[b_scdA[p2], b_scdB[p2]])
            if P2N < 4:
                return
            kb.op("act", lambda e, p2=p2: e.activation(
                expTS[:, 1024 * p2:1024 * (p2 + 1)],
                A(scd[p2], 0, (1024, 128), (512, 2), (128, 4), (1, 128)),
                AF.Exp, scale=RSCALE),
                reads=[b_scdA[p2], b_scdB[p2]], writes=[b_expTS[p2]])

        def p2_av(sg, t0):
            p2 = sg % 2
            if P2N < 5:
                return
            def av8_fn(e, sg=sg, t0=t0, p2=p2):
                last = None
                for j in range(4):
                    for t2 in range(2):
                        tok = t0 + 8 * sg + 2 * j + t2
                        last = e.matmul(
                            A(pb[p2], 9 * (2 * j + t2), (512, 128), (1, 9)),
                            expTS[:, 1024 * p2 + 512 * t2 + 128 * j:
                                  1024 * p2 + 512 * t2 + 128 * j + 128],
                            A(vsl, tok, (9 * T, 128), (T, 9)),
                            start=True, stop=True)
                return last
            kb.op("pe", av8_fn, reads=[b_expTS[p2], b_vsl], writes=[b_pb[p2]])
            if P2N < 6:
                return
            tb = t0 + 8 * sg
            kb.op("dve", lambda e, p2=p2: e.reciprocal(
                rc[:, 8 * p2:8 * p2 + 8], A(pb[p2], 8, (512, 128), (9, 8))),
                reads=[b_pb[p2]], writes=[b_rc[p2]])
            kb.op("dve", lambda e, p2=p2, tb=tb: e.tensor_tensor(
                A(ssl, tb, (KV * T, 128), (1, 8), (T, 8)),
                A(pb[p2], 0, (512, 128), (9, 8), (1, 8)),
                A(rc, 8 * p2, (16, 128), (1, 8), (0, 8)),
                ALU.mult),
                reads=[b_pb[p2], b_rc[p2]], writes=[b_ssl])

        for sg in range(16):
            p2_tp(sg)
            if sg >= 1:
                p2_sc(sg - 1)
            if sg >= 2:
                p2_av(sg - 2, t0)
        p2_sc(15)
        p2_av(14, t0)
        p2_av(15, t0)

    ffnb = [[(pb[0], 0, 512, b_pb[0]), (pb[1], 0, 512, b_pb[1])],
            [(scd[0], 512, 1024, b_scdB[0]), (scd[1], 512, 1024, b_scdB[1])]]

    def p3_epilogue(c):
        cp = c % 2
        for half in range(NH):
            bt_, off_, pp_, bb_ = ffnb[cp][half]
            kb.op("act", lambda e, half=half, bt_=bt_, off_=off_, pp_=pp_: e.activation(
                A(gsb, 512 * half, (1024, 128), (1, HW)),
                A(bt_, off_, (pp_, 128), (1, HW)), AF.Silu),
                reads=[bb_], writes=[b_g[half]])
            kb.op("dve", lambda e, half=half, c=c: e.tensor_tensor(
                A(sgs, 512 * half, (1024, 128), (1, HW)),
                A(gsb, 512 * half, (1024, 128), (1, HW)),
                A(ssl, (c // 4) * T + half * HW, (KV * T, 128), (1, HW)),
                ALU.mult),
                reads=[b_g[half], b_ssl], writes=[b_sg[half]])
            def tp_fn(e, half=half):
                last = None
                for i in range(HW // 128):
                    last = e.transpose(
                        A(scd[half], 128 * i, (1024, 128), (1, 128)),
                        A(sgs, 512 * half + 128 * i, (1024, 128), (1, 128)),
                        idf[:, :])
                return last
            kb.op("pe", tp_fn, reads=[b_sg[half], b_idf], writes=[b_scdA[half]])
            kb.op("dve", lambda e, half=half: e.tensor_copy(
                A(gsb, 512 * half, (1024, 128), (1, HW)),
                A(scd[half], 0, (1024, 128), (1, HW))),
                reads=[b_scdA[half]], writes=[b_sge[half]])
            kb.op("pool", lambda e, c=c, half=half: e.dma_start(
                out=d_sg[half * HW:(half + 1) * HW, 128 * c:128 * (c + 1)].rearrange(
                    "(q p) d -> p q d", p=128),
                in_=A(gsb, 512 * half, (1024, 128), (128, HW // 128), (1, 128))),
                reads=[b_sge[half]], writes=[b_sgscr[c]], dma=f"sgscr{half}")

    xts = [(xt, b_xt), (xtp, b_xtp)] + ([(xtv, b_xtv)] if xtv is not None else [])
    NXB = len(xts)

    def p4_xload(tt):
        xb, bxb = xts[tt % NXB]
        eng = "act" if tt % 2 == 0 else "sp"
        kb.op(eng, lambda e, tt=tt, xb=xb: e.dma_start(
            out=xb[:, :], in_=d_x[128 * tt:128 * (tt + 1), :]),
            writes=[bxb], dma=f"xtb{tt % NXB}")

    for c in (range(OC) if "3" in phases else ()):
        par = pari % 2
        pari += 1
        cp = c % 2
        if c == 2 and "4" in phases:
            for t_ in range(min(NXB, NT)):
                p4_xload(t_)
        kb.op("sp", lambda e, c=c, par=par: e.dma_start(
            out=lhsT[par][:, :], in_=d_ccout[128 * c:128 * (c + 1), :]),
            reads=[b_ccout], writes=[b_lhs[par]], dma=f"lhss{par}")
        for half in range(NH):
            bt_, off_, pp_, bb_ = ffnb[cp][half]
            def ffn_fn(e, par=par, half=half, c=c, bt_=bt_, off_=off_, pp_=pp_):
                for k in range(NE):
                    e.matmul(A(bt_, off_, (pp_, 128), (1, HW)),
                             lhsT[par][:, 128 * k:128 * (k + 1)],
                             A(xnTh[half % NXH], k * TH, (NE * TH, 128), (1, HW)),
                             start=(k == 0), stop=False)
                return e.matmul(A(bt_, off_, (pp_, 128), (1, HW)),
                                Bt[:, 128 * c:128 * (c + 1)],
                                lm[:, half * HW:(half + 1) * HW],
                                start=False, stop=True)
            kb.op("pe", ffn_fn,
                  reads=[b_lhs[par], b_xnTh[half % NXH], b_Bt, b_lm],
                  writes=[bb_])
        if c > 0:
            p3_epilogue(c - 1)
    if "3" in phases:
        p3_epilogue(OC - 1)

    for tt in (range(NT) if "4" in phases else ()):
        par = tt % 2
        kb.op("sp" if par == 0 else "act", lambda e, tt=tt, par=par: e.dma_start(
            out=sgT[par][:, :], in_=d_sg[128 * tt:128 * (tt + 1), :]),
            reads=b_sgscr, writes=[b_sgT[par]], dma=f"sgT{par}")
        xb, bxb = xts[tt % NXB]
        for half in range(2):
            op = (tt * 2 + half) % 2
            kb.op("dve", lambda e, par=par, half=half, op=op, xb=xb, bxb=bxb:
              e.tensor_tensor(
                A(outh[op], 0, (2048, 128), (32, 64), (1, 32)),
                A(sgT[par], 64 * half, (4096, 128), (1, 64), (128, 32)),
                A(xb, 2048 * half, (E, 128), (32, 64), (1, 32)),
                ALU.add),
                reads=[b_sgT[par], bxb], writes=[b_outh[op]])
            kb.op("pool",
                  lambda e, tt=tt, half=half, op=op: e.dma_start(
                out=d_y[128 * tt:128 * (tt + 1), 2048 * half:2048 * (half + 1)],
                in_=outh[op][:, :]),
                reads=[b_outh[op]], writes=[b_ydma[tt * 2 + half]],
                dma=f"y{op}")
        if tt + NXB < NT:
            p4_xload(tt + NXB)

    if "4" in phases:
        kb.op("sp", lambda e: e.nop(), reads=b_ydma)
    else:
        kb.op("sp", lambda e: e.dma_start(out=d_y[0:128, 0:16], in_=bkv[:, :]),
              reads=[b_bkv], writes=[b_ydma[0]], dma="y0")
        kb.op("sp", lambda e: e.nop(), reads=[b_ydma[0]])

    sems = {}
    semguards = []
    semkeys = [("c", x) for x in ENGS] + sorted(kb.dcount.keys(), key=str)
    with nc.allow_low_precision("bf16 matmul operands / bf16 stream by design"):
        for key in semkeys:
            g = nc.semaphore(f"s_{key[0]}_{key[1]}")
            semguards.append(g)
            sems[key] = g.__enter__()
        blk = nc.Block()
        block = blk.__enter__()
        kb.emit(block, sems)
        blk.__exit__(None, None, None)
    for g in reversed(semguards):
        g.__exit__(None, None, None)
    for g in reversed(guards):
        g.__exit__(None, None, None)
    return nc


def shard_inputs(L0_attn_x, L0_attn_Wq, L0_attn_bq, L0_attn_Wk, L0_attn_bk,
                 L0_attn_Wv, L0_attn_bv, L0_ffn_packed_weight, L0_ffn_block_scales,
                 L0_ffn_global_scale, L0_ffn_lora_A, L0_ffn_lora_B,
                 T=1024, ncores=8):
    x = np.ascontiguousarray(np.asarray(L0_attn_x, np.float32).reshape(-1, E))
    wq = np.asarray(L0_attn_Wq, np.float32).reshape(E, 128, 8, 4)
    wq_p = wq.transpose(2, 0, 1, 3).reshape(8, E, 512)
    wq_p = np.ascontiguousarray(
        wq_p.reshape(8, 8, 4, 128, 512).transpose(0, 1, 3, 2, 4).reshape(8, 8, 128, 2048))
    bq = np.ascontiguousarray(np.asarray(L0_attn_bq, np.float32))
    wk_p = np.asarray(L0_attn_Wk, np.float32).transpose(2, 0, 1)
    wk_p = np.ascontiguousarray(
        wk_p.reshape(8, 32, 128, 128).transpose(0, 2, 1, 3).reshape(8, 128, 4096))
    bk = np.ascontiguousarray(np.asarray(L0_attn_bk, np.float32))
    wv_p = np.asarray(L0_attn_Wv, np.float32).transpose(2, 0, 1)
    wv_p = np.ascontiguousarray(
        wv_p.reshape(8, 32, 128, 128).transpose(0, 2, 1, 3).reshape(8, 128, 4096))
    bv = np.ascontiguousarray(np.asarray(L0_attn_bv, np.float32))
    pk = np.asarray(L0_ffn_packed_weight, np.int32)
    sc = np.asarray(L0_ffn_block_scales, np.float32)
    gs = np.asarray(L0_ffn_global_scale, np.float32).reshape(1, 1)
    la = np.ascontiguousarray(np.asarray(L0_ffn_lora_A, np.float32))
    lb = np.asarray(L0_ffn_lora_B, np.float32)
    lb = np.ascontiguousarray(lb.reshape(32, 128, 16).transpose(1, 0, 2).reshape(128, 512))
    nh = 32 // ncores
    in_maps = []
    for r in range(ncores):
        idx = np.array([d * 32 + nh * r + h1 for h1 in range(nh) for d in range(128)])
        in_maps.append({
            "x_in": np.ascontiguousarray(x[r * T:(r + 1) * T]),
            "wq_p": wq_p, "bq_in": bq,
            "wk_p": wk_p, "bk_in": bk,
            "wv_p": wv_p, "bv_in": bv,
            "pk_in": np.ascontiguousarray(pk[idx]),
            "sc_in": np.ascontiguousarray(sc[idx]),
            "gs_in": gs, "a_in": la, "b_in": lb,
        })
    return in_maps


def kernel(**inputs):
    T, ncores = 1024, 8
    in_maps = shard_inputs(T=T, ncores=ncores, **inputs)
    nc = build_program(T, ncores)
    res = run_bass_kernel_spmd(nc, in_maps, core_ids=list(range(ncores)))
    y = np.concatenate([np.asarray(res.results[r]["y_out"], np.float32)
                        for r in range(ncores)], 0)
    return y.reshape(1, ncores * T, E)
